# version 02
# baseline (speedup 1.0000x reference)
import os
import numpy as np
import ml_dtypes
import concourse.bass as bass
import concourse.mybir as mybir
from concourse.bass_utils import run_bass_kernel_spmd
from contextlib import ExitStack

F32 = mybir.dt.float32
BF16 = mybir.dt.bfloat16
AF = mybir.ActivationFunctionType
ALU = mybir.AluOpType
AX = mybir.AxisListType

H, D, S, DM = 16, 128, 2048, 2048
NB, BLK = 32, 64
HFF = 8192
NJ = 8
SM = float(D) ** -0.5
NEG = -1e31
EPS_LN = 1e-5
EPS_RMS = 1.1920929e-07
bfdt = ml_dtypes.bfloat16

ENGS = ("sync", "scalar", "vector", "tensor", "gpsimd")
RING = {"sync": "qS", "scalar": "qA", "gpsimd": "qG"}


DEBUG = os.environ.get("KDBG", "") == "1"


def build_nc():
    nc = bass.Bass("TRN2", target_bir_lowering=False, debug=False)

    xb = nc.dram_tensor("xb", [S, DM], F32, kind="ExternalInput").ap()
    wq = nc.dram_tensor("wq", [DM, DM], F32, kind="ExternalInput").ap()
    wk = nc.dram_tensor("wk", [DM, DM], F32, kind="ExternalInput").ap()
    wv = nc.dram_tensor("wv", [DM, DM], F32, kind="ExternalInput").ap()
    wo = nc.dram_tensor("wo", [DM, DM], F32, kind="ExternalInput").ap()
    wgu = nc.dram_tensor("wgu", [2 * HFF, DM], F32, kind="ExternalInput").ap()
    wpr = nc.dram_tensor("wpr", [DM, HFF], F32, kind="ExternalInput").ap()
    gmi = nc.dram_tensor("gm", [NJ, 128, NB], F32, kind="ExternalInput").ap()
    cmi = nc.dram_tensor("cmask", [NJ, 128, NB], F32, kind="ExternalInput").ap()
    i128 = nc.dram_tensor("i128", [128, 128], BF16, kind="ExternalInput").ap()
    idf = nc.dram_tensor("idf", [128, 128], F32, kind="ExternalInput").ap()
    tri = nc.dram_tensor("tri", [128, 128], BF16, kind="ExternalInput").ap()
    ones1 = nc.dram_tensor("ones1", [128, 1], BF16, kind="ExternalInput").ap()
    y = nc.dram_tensor("y", [NJ * 128, DM], F32, kind="ExternalOutput").ap()

    ln1s = nc.dram_tensor("ln1s", [S, DM], BF16, kind="Internal").ap()
    vts = nc.dram_tensor("vts", [DM, S], BF16, kind="Internal").ap()
    ots = nc.dram_tensor("ots", [H, 128, 1024], BF16, kind="Internal").ap()
    x1s = nc.dram_tensor("x1s", [1024, DM], F32, kind="Internal").ap()
    ln2s = nc.dram_tensor("ln2s", [1024, DM], BF16, kind="Internal").ap()
    wgubT = nc.dram_tensor("wgubT", [DM, 2 * HFF], BF16, kind="Internal").ap()
    wprbT = nc.dram_tensor("wprbT", [HFF, DM], BF16, kind="Internal").ap()
    sscr = nc.dram_tensor("sscr", [2 * HFF, 128], BF16, kind="Internal").ap()
    sscrp = nc.dram_tensor("sscrp", [DM, 128], BF16, kind="Internal").ap()
    sscrt = nc.dram_tensor("sscrt", [32, 2 * HFF], BF16, kind="Internal").ap()
    hhb = nc.dram_tensor("hhb", [1024, HFF], BF16, kind="Internal").ap()
    if DEBUG:
        dbg_ln1 = nc.dram_tensor("dbg_ln1", [S, DM], BF16, kind="ExternalOutput").ap()
        dbg_vts = nc.dram_tensor("dbg_vts", [DM, S], BF16, kind="ExternalOutput").ap()
        dbg_ots = nc.dram_tensor("dbg_ots", [H, 128, 1024], BF16, kind="ExternalOutput").ap()
        dbg_x1 = nc.dram_tensor("dbg_x1", [1024, DM], F32, kind="ExternalOutput").ap()
        dbg_ln2 = nc.dram_tensor("dbg_ln2", [1024, DM], BF16, kind="ExternalOutput").ap()
        dbg_h = nc.dram_tensor("dbg_h", [1024, HFF], BF16, kind="ExternalOutput").ap()
        dbg_ss = nc.dram_tensor("dbg_ss", [2 * HFF, 128], BF16, kind="ExternalOutput").ap()
        dbg_wgub = nc.dram_tensor("dbg_wgub", [DM, 2 * HFF], BF16, kind="ExternalOutput").ap()
        dbg_sig = nc.dram_tensor("dbg_sig", [H, 1024], F32, kind="ExternalOutput").ap()
        dbg_gb = nc.dram_tensor("dbg_gb", [H, 32, 1024], BF16, kind="ExternalOutput").ap()
        dbg_inv = nc.dram_tensor("dbg_inv", [128, 8], F32, kind="ExternalOutput").ap()
        dbg_k = nc.dram_tensor("dbg_k", [128, 2048], BF16, kind="ExternalOutput").ap()
        dbg_kg = nc.dram_tensor("dbg_kg", [128, 32], BF16, kind="ExternalOutput").ap()
        dbg_q = nc.dram_tensor("dbg_q", [128, 1024], BF16, kind="ExternalOutput").ap()
        dbg_gsb = nc.dram_tensor("dbg_gsb", [128, 256], F32, kind="ExternalOutput").ap()
        dbg_th8 = nc.dram_tensor("dbg_th8", [128, 64], F32, kind="ExternalOutput").ap()
        dbg_sigr = nc.dram_tensor("dbg_sigr", [H, 1024], F32, kind="ExternalOutput").ap()
        dbg_pt = nc.dram_tensor("dbg_pt", [16, 128, 1024], BF16, kind="ExternalOutput").ap()
        dbg_me = nc.dram_tensor("dbg_me", [16, 128, 1024], BF16, kind="ExternalOutput").ap()
        dbg_pt2 = nc.dram_tensor("dbg_pt2", [16, 128, 1024], BF16, kind="ExternalOutput").ap()
        dbg_me2 = nc.dram_tensor("dbg_me2", [2, 128, 1024], BF16, kind="ExternalOutput").ap()
        dbg_wq = nc.dram_tensor("dbg_wq", [16, 128, 512], BF16, kind="ExternalOutput").ap()
        dbg_se = nc.dram_tensor("dbg_se", [16, 128, 512], BF16, kind="ExternalOutput").ap()
        dbg_gu = nc.dram_tensor("dbg_gu", [8, 128, 512], BF16, kind="ExternalOutput").ap()
        dbg_sgb = nc.dram_tensor("dbg_sgb", [8, 128, 512], BF16, kind="ExternalOutput").ap()
        dbg_up = nc.dram_tensor("dbg_up", [8, 128, 512], BF16, kind="ExternalOutput").ap()
        dbg_ht = nc.dram_tensor("dbg_ht", [8, 128, 512], BF16, kind="ExternalOutput").ap()
        dbg_wqc = nc.dram_tensor("dbg_wqc", [128, 8 * 2048], BF16, kind="ExternalOutput").ap()
        dbg_htc = nc.dram_tensor("dbg_htc", [128, 8 * 1024], BF16, kind="ExternalOutput").ap()
        dbg_acc = nc.dram_tensor("dbg_acc", [128, 8 * 2048], F32, kind="ExternalOutput").ap()
        dbg_acc0 = nc.dram_tensor("dbg_acc0", [128, 8 * 2048], F32, kind="ExternalOutput").ap()
        dbg_prq = nc.dram_tensor("dbg_prq", [2, 128, 2048], BF16, kind="ExternalOutput").ap()
        dbg_sep = nc.dram_tensor("dbg_sep", [2, 128, 2048], BF16, kind="ExternalOutput").ap()
        dbg_pqa = nc.dram_tensor("dbg_pqa", [2, 128, 2048], BF16, kind="ExternalOutput").ap()

    plan = {e: [] for e in ENGS}
    cnt = {"qS": 0, "qA": 0, "qG": 0, "qM": 0, "qW": 0, "qT": 0, "qY": 0,
           "eT": 0, "eV": 0, "eS": 0, "eG": 0}
    RINGSEMS = ("qS", "qA", "qG", "qM", "qW", "qT", "qY")
    last_wait = {e: {k: 0 for k in cnt} for e in ENGS}
    semh = {}

    def emit(eng, fn):
        plan[eng].append(fn)

    def wait(eng, tok):
        if tok is None:
            return
        s, v = tok
        if s in RINGSEMS:
            v = cnt[s]
        if v <= 0 or last_wait[eng][s] >= v:
            return
        last_wait[eng][s] = v
        emit(eng, lambda E, s=s, v=v: E.wait_ge(semh[s], v))

    def waits(eng, *toks):
        for t in toks:
            wait(eng, t)

    def dma(eng, out, in_, transpose=False, sem=None):
        ring = sem if sem is not None else RING[eng]
        cnt[ring] += 16
        v = cnt[ring]

        def f(E, out=out, in_=in_, tr=transpose, s=ring):
            E.dma_start(out=out, in_=in_, transpose=tr).then_inc(semh[s], 16)
        emit(eng, f)
        return (ring, v)

    def bar(eng):
        emit(eng, lambda E: E.drain())

    def opd(eng, fn, sem):
        op(eng, fn)
        return op(eng, lambda E: E.drain(), sem)

    def op(eng, fn, sem=None):
        tok = None
        if sem is not None:
            cnt[sem] += 1
            tok = (sem, cnt[sem])

        def f(E, fn=fn, tok=tok):
            ins = fn(E)
            if tok is not None:
                ins.then_inc(semh[tok[0]], 1)
        emit(eng, f)
        return tok

    root = ExitStack()

    def mksb(st, name, shape, dtype, side=None):
        if side is None:
            return st.enter_context(nc.sbuf_tensor(name, shape, dtype))
        return st.enter_context(nc.sbuf_tensor(name, shape, dtype, side=side))

    def mkps(st, name, shape):
        return st.enter_context(nc.psum_tensor(name, shape, F32))

    with root:
        for nm in ("qS", "qA", "qG", "qM", "qW", "qT", "qY",
                   "eT", "eV", "eS", "eG"):
            semh[nm] = root.enter_context(nc.semaphore(nm))

        c_i128 = mksb(root, "c_i128", [128, 128], BF16)
        c_idf = mksb(root, "c_idf", [128, 128], F32)
        c_tri = mksb(root, "c_tri", [128, 128], BF16)
        c_ones = mksb(root, "c_ones", [128, 1], BF16)
        gm_sb = mksb(root, "gm_sb", [128, NJ * 32], F32)
        cm_sb = mksb(root, "cm_sb", [128, NJ * 32], F32)
        stats = mksb(root, "stats", [128, 32], F32)
        junkb = mksb(root, "junkb", [128, 2048], BF16)
        ssqp = mksb(root, "ssqp", [128, 128], F32)
        invj = mksb(root, "invj", [128, 8], F32)

        dma("sync", c_i128[:], i128)
        dma("sync", c_idf[:], idf)
        dma("sync", c_tri[:], tri)
        dma("sync", c_ones[:], ones1)
        dma("sync", gm_sb[:], bass.AP(gmi.tensor, 0, [[32, 128], [128 * 32, NJ], [1, 32]]))
        tok_consts = dma("sync", cm_sb[:], bass.AP(cmi.tensor, 0, [[32, 128], [128 * 32, NJ], [1, 32]]))

        esA = ExitStack()
        ln1xt = mksb(esA, "ln1xt", [128, 16 * 2048], BF16)
        vall = mksb(esA, "vall", [128, 16 * 2048], BF16)
        cp_nat = [mksb(esA, f"cp_nat{i}", [128, 2048], BF16) for i in range(2)]
        cp_abs = mksb(esA, "cp_abs", [128, 2048], BF16)
        cp_sum = mksb(esA, "cp_sum", [128, 32], F32)
        cp_s = mksb(esA, "cp_s", [128, 32], BF16)
        cp_stage = [mksb(esA, f"cp_stage{i}", [128, 1024], BF16) for i in range(2)]
        psTR = ExitStack()
        pstr = [psTR.enter_context(nc.psum_tensor(f"pstr{i}", [128, 1024], BF16))
                for i in range(2)]

        cp_busy = [None, None]
        cp_busy2 = [None, None]
        cp_abs_used = [None]
        cp_sum_used = [None]
        cp_s_used = [None]
        cp_pos = [0]
        cp_last_store = [None]
        tbank_free = [None, None]
        stage_free = [None, None]
        tbank_i = [0]
        stage_i = [0]

        def castpass_unit(u):
            buf = u % 2
            if u < 128:
                src = wgu[u * 128:(u + 1) * 128, :]
                sdst = sscr[u * 128:(u + 1) * 128, 0:32]
                trow = 2 * HFF
                toff0 = u * 128
            else:
                v0 = u - 128
                dt, pc = v0 // 4, v0 % 4
                src = wpr[dt * 128:(dt + 1) * 128, pc * 2048:(pc + 1) * 2048]
                sdst = sscrp[dt * 128:(dt + 1) * 128, pc * 32:(pc + 1) * 32]
                trow = DM
                toff0 = None
            waits("gpsimd", cp_busy[buf], cp_busy2[buf])
            tin = dma("gpsimd", cp_nat[buf][:], src)
            waits("scalar", tin, cp_abs_used[0])
            ta = op("scalar", lambda E, b=buf: E.activation(
                cp_abs[:], cp_nat[b][:], AF.Abs), "eS")
            red_eng = "vector"
            waits(red_eng, ta, cp_sum_used[0])
            a3 = cp_abs[:].rearrange("p (g c) -> p g c", c=64)
            tr = op(red_eng, lambda E, a3=a3: E.tensor_reduce(
                cp_sum[:], a3, AX.X, ALU.add),
                "eV" if red_eng == "vector" else "eG")
            cp_abs_used[0] = tr
            waits("vector", tr, cp_s_used[0])
            bar("vector")
            tm = opd("vector", lambda E: E.tensor_scalar(
                cp_s[:], cp_sum[:], 1.0 / 64.0, 1e-8, ALU.mult, ALU.max), "eV")
            cp_sum_used[0] = tm
            waits("sync", tm)
            cp_s_used[0] = dma("sync", sdst, cp_s[:])
            cp_last_store[0] = cp_s_used[0]
            ttr = None
            for halfu in range(2):
                bk = tbank_i[0]
                tbank_i[0] ^= 1
                waits("tensor", tok_consts, tin, tbank_free[bk])
                for t8 in range(8):
                    t = halfu * 8 + t8
                    ttr = op("tensor", lambda E, bk=bk, t8=t8, t=t, b=buf: E.transpose(
                        pstr[bk][:, t8 * 128:(t8 + 1) * 128],
                        cp_nat[b][:, t * 128:(t + 1) * 128], c_i128[:]),
                        "eT" if t8 == 7 else None)
                sg = stage_i[0]
                stage_i[0] ^= 1
                waits("vector", ttr, stage_free[sg])
                tev = opd("vector", lambda E, sg=sg, bk=bk: E.tensor_copy(
                    cp_stage[sg][:], pstr[bk][:]), "eV")
                tbank_free[bk] = tev
                waits("gpsimd", tev)
                if u < 128:
                    doff = halfu * 1024 * trow + u * 128
                    ddst = bass.AP(wgubT.tensor, doff,
                                   [[trow, 128], [128 * trow, 8], [1, 128]])
                else:
                    doff = (pc * 2048 + halfu * 1024) * trow + dt * 128
                    ddst = bass.AP(wprbT.tensor, doff,
                                   [[trow, 128], [128 * trow, 8], [1, 128]])
                tst = dma("gpsimd", ddst,
                          bass.AP(cp_stage[sg], 0, [[1024, 128], [128, 8], [1, 128]]))
                stage_free[sg] = tst
            cp_busy[buf] = ta
            cp_busy2[buf] = ttr

        def castpass_step(n):
            for _ in range(n):
                if cp_pos[0] >= 192:
                    return
                castpass_unit(cp_pos[0])
                cp_pos[0] += 1

        esP1 = ExitStack()
        xts = [mksb(esP1, f"xt{i}", [128, 2048], F32) for i in range(2)]
        lsts = [mksb(esP1, f"lst{i}", [128, 2048], BF16) for i in range(2)]
        ap_done = [None, None]
        st_done = [None, None]
        prev_tv = None
        for tc in range(16):
            b = tc % 2
            waits("sync", ap_done[b])
            tl = dma("sync", xts[b][:], xb[tc * 128:(tc + 1) * 128, :])
            waits("scalar", tl, prev_tv)
            t1 = op("scalar", lambda E, b=b: E.activation(
                junkb[:], xts[b][:], AF.Identity, accum_out=stats[:, 0:1]), "eS")
            t2 = op("scalar", lambda E, b=b: E.activation(
                junkb[:], xts[b][:], AF.Square, accum_out=stats[:, 1:2]), "eS")
            waits("vector", t2)
            op("vector", lambda E: E.tensor_scalar(
                stats[:, 2:3], stats[:, 0:1], 1.0 / 2048.0, None, ALU.mult))
            op("vector", lambda E: E.tensor_scalar(
                stats[:, 3:4], stats[:, 1:2], 1.0 / 2048.0, None, ALU.mult))
            op("vector", lambda E: E.tensor_mul(
                stats[:, 4:5], stats[:, 2:3], stats[:, 2:3]))
            bar("vector")
            op("vector", lambda E: E.tensor_sub(
                stats[:, 5:6], stats[:, 3:4], stats[:, 4:5]))
            bar("vector")
            tv = op("vector", lambda E: E.tensor_scalar(
                stats[:, 5:6], stats[:, 5:6], 1.0, EPS_LN, ALU.mult, ALU.add), "eV")
            prev_tv = tv
            waits("scalar", tv)
            tq = op("scalar", lambda E: E.activation(
                stats[:, 6:7], stats[:, 5:6], AF.Sqrt), "eS")
            waits("vector", tq)
            op("vector", lambda E: E.reciprocal(stats[:, 7:8], stats[:, 6:7]))
            bar("vector")
            op("vector", lambda E: E.scalar_tensor_tensor(
                stats[:, 8:9], stats[:, 2:3], -1.0, stats[:, 7:8],
                ALU.mult, ALU.mult))
            bar("vector")
            waits("vector", st_done[b])
            ta = opd("vector", lambda E, b=b: E.tensor_scalar(
                lsts[b][:], xts[b][:], stats[:, 7:8], stats[:, 8:9],
                ALU.mult, ALU.add), "eV")
            ap_done[b] = ta
            waits("sync", ta)
            st_done[b] = dma("sync", ln1s[tc * 128:(tc + 1) * 128, :], lsts[b][:])
            if tc % 3 == 2:
                castpass_step(1)
        tok_ln1_stores = st_done[:]
        esP1.close()

        waits("scalar", tok_ln1_stores[0], tok_ln1_stores[1], tok_consts)
        tok_ln1xt = None
        for dt in range(16):
            tok_ln1xt = dma("scalar", ln1xt[:, dt * 2048:(dt + 1) * 2048],
                            ln1s[:, dt * 128:(dt + 1) * 128], transpose=True, sem="qT")

        esPS2 = ExitStack()
        psa = mkps(esPS2, "psa", [128, 512])
        psb = mkps(esPS2, "psb", [128, 512])
        psot = mkps(esPS2, "psot", [128, 1024])
        pssig = mkps(esPS2, "pssig", [1, 1024])

        esP2p = ExitStack()
        wvcs = [mksb(esP2p, f"wvc{i}", [128, 2048], BF16) for i in range(2)]
        vtsts = [mksb(esP2p, f"vtst{i}", [128, 1024], BF16) for i in range(2)]
        wv_used = [None, None]
        vt_stored = [None, None]
        ps_free = [None] * 4
        slices4 = [(psa, 0), (psb, 0), (psa, 0), (psb, 0)]
        vstore = None
        for vc in range(16):
            b = vc % 2
            waits("gpsimd", wv_used[b])
            tw = dma("gpsimd", wvcs[b][:],
                     bass.AP(wv.tensor, vc * 128,
                             [[2048, 128], [128 * 2048, 16], [1, 128]]))
            for cc in range(4):
                pst, off = slices4[cc]
                waits("tensor", tw, tok_ln1xt, ps_free[cc], ps_free[(cc + 2) % 4])
                lastmm = None
                for dt in range(16):
                    lastmm = op("tensor", lambda E, b=b, dt=dt, cc=cc, pst=pst, off=off: E.matmul(
                        pst[:, off:off + 512],
                        wvcs[b][:, dt * 128:(dt + 1) * 128],
                        ln1xt[:, dt * 2048 + cc * 512: dt * 2048 + (cc + 1) * 512],
                        start=(dt == 0), stop=(dt == 15)),
                        "eT" if dt == 15 else None)
                wv_used[b] = lastmm
                waits("scalar", lastmm)
                if cc == 0:
                    waits("scalar", vt_stored[b])
                if cc == 2:
                    waits("scalar", vt_stored[b])
                tev = opd("scalar", lambda E, b=b, cc=cc, pst=pst, off=off: E.activation(
                    vtsts[b][:, (cc % 2) * 512:(cc % 2) * 512 + 512],
                    pst[:, off:off + 512], AF.Identity), "eS")
                ps_free[cc] = tev
                if cc % 2 == 1:
                    waits("sync", tev)
                    half = cc // 2
                    vt_stored[b] = dma(
                        "sync",
                        vts[vc * 128:(vc + 1) * 128, half * 1024:(half + 1) * 1024],
                        vtsts[b][:, 0:1024])
                    vstore = vt_stored[b]
            castpass_step(2)
        esP2p.close()
        waits("scalar", vstore)
        tok_vall = None
        for kt in range(16):
            tok_vall = dma("scalar", vall[:, kt * 2048:(kt + 1) * 2048],
                           vts[:, kt * 128:(kt + 1) * 128], transpose=True, sem="qT")

        esB = ExitStack()
        wqh = [mksb(esB, f"wqh{i}", [128, 2048], BF16) for i in range(2)]
        wkh = [mksb(esB, f"wkh{i}", [128, 2048], BF16) for i in range(2)]
        ksb = mksb(esB, "ksb", [128, 2048], BF16)
        qsb = mksb(esB, "qsb", [128, 1024], BF16)
        kga = mksb(esB, "kga", [128, 1024], F32)
        kgb = mksb(esB, "kgb", [128, 512], F32)
        kgf = mksb(esB, "kgf", [128, 32], BF16)
        gsb = mksb(esB, "gsb", [128, 256], F32)
        th8 = mksb(esB, "th8", [128, 64], F32)
        msk = mksb(esB, "msk", [128, 256], F32)
        gb01 = mksb(esB, "gb01", [32, 1024], BF16)
        ptb = [mksb(esB, f"ptb{i}", [128, 1024], BF16) for i in range(2)]
        meb = [mksb(esB, f"meb{i}", [128, 1024], BF16) for i in range(4)]
        ceb = mksb(esB, "ceb", [128, 1024], F32)
        csb = mksb(esB, "csb", [1, 1024], F32)
        otst = mksb(esB, "otst", [128, 1024], BF16)

        wq_tok = [None, None]
        wq_used = [None, None]

        def prefetch_head(h_):
            b = h_ % 2
            waits("gpsimd", wq_used[b])
            dma("gpsimd", wqh[b][:],
                bass.AP(wq.tensor, h_ * 128,
                        [[2048, 128], [128 * 2048, 16], [1, 128]]))
            wq_tok[b] = dma("gpsimd", wkh[b][:],
                            bass.AP(wk.tensor, h_ * 128,
                                    [[2048, 128], [128 * 2048, 16], [1, 128]]))

        prefetch_head(0)

        pend_ot = [None]

        def flush_ot():
            if pend_ot[0] is None:
                return
            hprev, tce_p = pend_ot[0]
            pend_ot[0] = None
            waits("vector", tce_p, ot_store_t[0])
            ote = opd("vector", lambda E: E.tensor_mul(
                otst[:], psot[:], ceb[:]), "eV")
            ot_evac_t[0] = ote
            waits("sync", ote)
            ot_store_t[0] = dma("sync", ots[hprev, :, :], otst[:])

        ot_evac_t = [None]
        ot_store_t = [None]

        exp_tok = [None, None]
        mask_tok = [None, None]
        sp_tok = [None, None]
        chslot = [0]
        c_evac = None
        gb_evac = None
        me_last = None
        tce_last = None
        dbg_sig_tok = [None]
        dbg_gb_tok = [None]
        dbg_pt_tok = [None, None]
        dbg_me_tok_n = [None] * 4
        dbg_pt2_tok = [None, None]
        me_mask_done = [None] * 4

        for h_ in range(H):
            b = h_ % 2
            if h_ + 1 < H:
                prefetch_head(h_ + 1)
            waits("tensor", wq_tok[b], exp_tok[0], exp_tok[1], tok_vall)
            kevac = [None] * 4
            for cc in range(4):
                pst, off = slices4[cc]
                if cc >= 2:
                    waits("tensor", kevac[cc - 2])
                lastmm = None
                for dt in range(16):
                    lastmm = op("tensor", lambda E, b=b, dt=dt, cc=cc, pst=pst, off=off: E.matmul(
                        pst[:, off:off + 512],
                        wkh[b][:, dt * 128:(dt + 1) * 128],
                        ln1xt[:, dt * 2048 + cc * 512: dt * 2048 + (cc + 1) * 512],
                        start=(dt == 0), stop=(dt == 15)),
                        "eT" if dt == 15 else None)
                waits("scalar", lastmm)
                kevac[cc] = op("scalar", lambda E, cc=cc, pst=pst, off=off: E.activation(
                    ksb[:, cc * 512:(cc + 1) * 512], pst[:, off:off + 512],
                    AF.Identity), "eS")
            qevac = [None, None]
            for jg in range(2):
                pst, off = slices4[jg]
                waits("tensor", kevac[jg + 2])
                lastmm = None
                for dt in range(16):
                    rhs = bass.AP(ln1xt, dt * 2048 + jg * 1024,
                                  [[16 * 2048, 128], [256, 4], [1, 128]])
                    lastmm = op("tensor", lambda E, b=b, rhs=rhs, dt=dt, pst=pst, off=off: E.matmul(
                        pst[:, off:off + 512],
                        wqh[b][:, dt * 128:(dt + 1) * 128], rhs,
                        start=(dt == 0), stop=(dt == 15)),
                        "eT" if dt == 15 else None)
                wq_used[b] = lastmm
                waits("scalar", lastmm)
                qevac[jg] = op("scalar", lambda E, jg=jg, pst=pst, off=off: E.activation(
                    qsb[:, jg * 512:(jg + 1) * 512], pst[:, off:off + 512],
                    AF.Identity), "eS")
                if jg == 1:
                    qevac[jg] = op("scalar", lambda E: E.drain(), "eS")
            waits("vector", kevac[0], kevac[1], kevac[2], kevac[3])
            kv = ksb[:].rearrange("p (g c) -> p g c", c=64)
            ka = kga[:].rearrange("p (g c) -> p g c", c=32)
            op("vector", lambda E, kv=kv, ka=ka: E.tensor_tensor(
                ka, kv[:, :, 0:32], kv[:, :, 32:64], ALU.add))
            bar("vector")
            for lvl in range(1, 6):
                w_ = 32 >> lvl
                src = kga if lvl % 2 == 1 else kgb
                dst = kgb if lvl % 2 == 1 else kga
                sv = src[:, 0:32 * 2 * w_].rearrange("p (g c) -> p g c", c=2 * w_)
                dv = dst[:, 0:32 * w_].rearrange("p (g c) -> p g c", c=w_)
                op("vector", lambda E, sv=sv, dv=dv, w_=w_: E.tensor_tensor(
                    dv, sv[:, :, 0:w_], sv[:, :, w_:2 * w_], ALU.add))
                bar("vector")
            tkg = opd("vector", lambda E: E.tensor_scalar(
                kgf[:], kgb[:, 0:32], 1.0 / 64.0, None, ALU.mult), "eV")
            waits("tensor", qevac[0], qevac[1], tkg, sp_tok[0], sp_tok[1])
            gmm = [None] * NJ
            for j in range(NJ):
                gmm[j] = op("tensor", lambda E, j=j: E.matmul(
                    psb[:, j * 32:(j + 1) * 32],
                    qsb[:, j * 128:(j + 1) * 128], kgf[:],
                    start=True, stop=True), "eT")
            for j in range(NJ):
                waits("vector", gmm[j])
                op("vector", lambda E, j=j: E.tensor_add(
                    gsb[:, j * 32:(j + 1) * 32], psb[:, j * 32:(j + 1) * 32],
                    gm_sb[:, j * 32:(j + 1) * 32]))
            for j in range(NJ):
                op("vector", lambda E, j=j: E.max(
                    out=th8[:, j * 8:(j + 1) * 8], in_=gsb[:, j * 32:(j + 1) * 32]))
            for j in range(NJ):
                op("vector", lambda E, j=j: E.tensor_scalar(
                    msk[:, j * 32:(j + 1) * 32], gsb[:, j * 32:(j + 1) * 32],
                    th8[:, j * 8 + 7:j * 8 + 8], None, ALU.is_ge))
            tmv_l = [None] * NJ
            for j in range(NJ):
                tmv_l[j] = op("vector", lambda E, j=j: E.tensor_mul(
                    msk[:, j * 32:(j + 1) * 32], msk[:, j * 32:(j + 1) * 32],
                    cm_sb[:, j * 32:(j + 1) * 32]), "eV")
            last_tmv = op("vector", lambda E: E.drain(), "eV")
            tmv_l = [last_tmv] * NJ
            waits("scalar", me_last, dbg_gb_tok[0])
            for j in range(NJ):
                waits("tensor", tmv_l[j])
                ttr = op("tensor", lambda E, j=j: E.transpose(
                    psa[0:32, 0:128], msk[:, j * 32:(j + 1) * 32], c_idf[:]), "eT")
                waits("scalar", ttr)
                gb_evac = op("scalar", lambda E, j=j: E.activation(
                    gb01[0:32, j * 128:(j + 1) * 128], psa[0:32, 0:128],
                    AF.Identity), "eS")
                if j + 1 < NJ:
                    waits("tensor", gb_evac)
            gb_evac = op("scalar", lambda E: E.drain(), "eS")
            flush_ot()
            pspair = [psa, psb]

            def issue_me_bcast(kc):
                par4 = kc % 4
                jm_ = kc // 2
                qw_ = 1024 - jm_ * 128
                waits("scalar", me_mask_done[par4], gb_evac, dbg_me_tok_n[par4])
                t = dma("scalar", meb[par4][:, 0:qw_],
                        bass.AP(gb01, 2 * kc * 1024 + jm_ * 128,
                                [[1024, 2], [0, 64], [1, qw_]]), sem="qM")
                me_tok_n[kc] = t
                return t

            me_tok_n = [None] * 16
            for kc0 in range(4):
                tlast_me = issue_me_bcast(kc0)
            me_last = tlast_me
            pend = None
            for kc in range(16):
                par = kc % 2
                jm = kc // 2
                qw = 1024 - jm * 128
                nch = (qw + 511) // 512
                ch_slot = [None] * nch
                ch_mm = [None] * nch
                ch_te = [None] * nch
                for c in range(nch):
                    off = c * 512
                    n = min(512, qw - off)
                    slot = chslot[0]
                    chslot[0] ^= 1
                    ch_slot[c] = slot
                    pst = pspair[slot]
                    waits("tensor", exp_tok[slot])
                    if kc == 0:
                        waits("tensor", gb_evac, last_tmv)
                    stmm = op("tensor", lambda E, kc=kc, jm=jm, off=off, n=n, pst=pst: E.matmul(
                        pst[:, 0:n],
                        ksb[:, kc * 128:(kc + 1) * 128],
                        qsb[:, jm * 128 + off: jm * 128 + off + n],
                        start=True, stop=(kc % 2 == 1),
                        skip_group_check=True),
                        None if (kc % 2 == 0 and c == 0) else "eT")
                    if kc % 2 == 0 and c == 0:
                        stmm = op("tensor", lambda E, pst=pst: E.matmul(
                            pst[:, 0:128], c_i128[:], c_tri[:],
                            start=False, stop=True, skip_group_check=True), "eT")
                    ch_mm[c] = stmm
                if pend is not None:
                    pkc, ppar, pjm, pqw = pend
                    waits("tensor", mask_tok[ppar])
                    if pkc == 0:
                        waits("tensor", c_evac, ot_evac_t[0])
                    off = 0
                    last = None
                    while off < pqw:
                        n = min(512, pqw - off)
                        op("tensor", lambda E, pkc=pkc, pjm=pjm, off=off, n=n, ppar=ppar: E.matmul(
                            pssig[0:1, pjm * 128 + off: pjm * 128 + off + n],
                            c_ones[:], ptb[ppar][:, off:off + n],
                            start=(pkc == 0), stop=(pkc == 15),
                            skip_group_check=True))
                        last = op("tensor", lambda E, pkc=pkc, pjm=pjm, off=off, n=n, ppar=ppar, h_=h_: E.matmul(
                            psot[:, pjm * 128 + off: pjm * 128 + off + n],
                            vall[:, pkc * 2048 + h_ * 128: pkc * 2048 + (h_ + 1) * 128],
                            ptb[ppar][:, off:off + n],
                            start=(pkc == 0), stop=(pkc == 15),
                            skip_group_check=True), "eT" if off + n >= pqw else None)
                        off += n
                    sp_tok[ppar] = last
                for c in range(nch):
                    off = c * 512
                    n = min(512, qw - off)
                    pst = pspair[ch_slot[c]]
                    if c == 0:
                        waits("scalar", ch_mm[c], sp_tok[par], dbg_pt_tok[par])
                    else:
                        waits("scalar", ch_mm[c])
                    te = opd("scalar", lambda E, par=par, off=off, n=n, pst=pst: E.activation(
                        ptb[par][:, off:off + n], pst[:, 0:n],
                        AF.Exp, scale=SM), "eS")
                    exp_tok[ch_slot[c]] = te
                    ch_te[c] = te
                if DEBUG and h_ == 0:
                    waits("sync", ch_te[-1])
                    dbg_pt2_tok[par] = dma("sync", dbg_pt2[kc, :, 0:qw], ptb[par][:, 0:qw])
                waits("vector", dbg_pt2_tok[par])
                par4 = kc % 4
                tmk = None
                for c in range(nch):
                    off = c * 512
                    n = min(512, qw - off)
                    waits("vector", ch_te[c], me_tok_n[kc])
                    tmk = opd("vector", lambda E, par=par, par4=par4, off=off, n=n: E.tensor_mul(
                        ptb[par][:, off:off + n], ptb[par][:, off:off + n],
                        meb[par4][:, off:off + n]), "eV")
                mask_tok[par] = tmk
                me_mask_done[par4] = tmk
                if DEBUG and h_ == 0:
                    waits("sync", tmk)
                    dbg_pt_tok[par] = dma("sync", dbg_pt[kc, :, 0:qw], ptb[par][:, 0:qw])
                    dbg_me_tok_n[par4] = dma("sync", dbg_me[kc, :, 0:qw], meb[par4][:, 0:qw])
                if kc + 4 < 16:
                    me_last = issue_me_bcast(kc + 4)
                pend = (kc, par, jm, qw)
            pkc, ppar, pjm, pqw = pend
            waits("tensor", mask_tok[ppar])
            off = 0
            while off < pqw:
                n = min(512, pqw - off)
                op("tensor", lambda E, off=off, n=n, ppar=ppar: E.matmul(
                    pssig[0:1, pjm * 128 + off: pjm * 128 + off + n],
                    c_ones[:], ptb[ppar][:, off:off + n],
                    start=False, stop=True, skip_group_check=True))
                last = op("tensor", lambda E, off=off, n=n, ppar=ppar, h_=h_: E.matmul(
                    psot[:, pjm * 128 + off: pjm * 128 + off + n],
                    vall[:, 15 * 2048 + h_ * 128: 15 * 2048 + (h_ + 1) * 128],
                    ptb[ppar][:, off:off + n],
                    start=False, stop=True, skip_group_check=True),
                    "eT" if off + n >= pqw else None)
                off += n
            sp_tok[ppar] = last
            waits("vector", sp_tok[0], sp_tok[1], tce_last, dbg_sig_tok[0])
            if DEBUG:
                tsr = opd("vector", lambda E: E.tensor_copy(
                    ceb[0:1, 0:1024], pssig[0:1, :]), "eV")
                waits("sync", tsr)
                tsrd = dma("sync", dbg_sigr[h_, :], ceb[0:1, 0:1024])
                waits("vector", tsrd)
                bar("vector")
            c_evac = opd("vector", lambda E: E.reciprocal(
                csb[0:1, :], pssig[0:1, :]), "eV")
            waits("scalar", c_evac, ot_evac_t[0])
            tce = dma("scalar", ceb[:],
                      bass.AP(csb, 0, [[1024, 1], [0, 128], [1, 1024]]), sem="qM")
            tce_last = tce
            pend_ot[0] = (h_, tce)
            if DEBUG and h_ == 0:
                waits("scalar", me_mask_done[0], me_mask_done[1])
                t_me2a = dma("scalar", meb[0][:, 0:1024],
                             bass.AP(gb01, 0, [[1024, 2], [0, 64], [1, 1024]]))
                t_me2b = dma("scalar", meb[1][:, 0:1024],
                             bass.AP(gb01, 2 * 1024, [[1024, 2], [0, 64], [1, 1024]]))
                waits("sync", t_me2a, t_me2b)
                tm2a = dma("sync", dbg_me2[0], meb[0][:, 0:1024])
                tm2b = dma("sync", dbg_me2[1], meb[1][:, 0:1024])
                waits("scalar", tm2a, tm2b)
                me_mask_done[0] = ("qS", tm2a[1])
                me_mask_done[1] = ("qS", tm2b[1])
            if DEBUG:
                waits("sync", c_evac, gb_evac)
                tdsig = dma("sync", dbg_sig[h_, :], csb[0:1, :])
                dbg_sig_tok[0] = tdsig
                tdgb = dma("sync", dbg_gb[h_, :, :], gb01[:])
                dbg_gb_tok[0] = tdgb
                if h_ == H - 1:
                    dma("sync", dbg_k, ksb[:])
                    dma("sync", dbg_kg, kgf[:])
                    dma("sync", dbg_q, qsb[:])
                    dma("sync", dbg_gsb, gsb[:])
                    dma("sync", dbg_th8, th8[:])
            castpass_step(12)
        flush_ot()
        castpass_step(200)
        tok_ot_final = ot_store_t[0]
        tok_cp_all = ("qS", cnt["qS"])
        tok_cpw = ("qG", cnt["qG"])

        esB.close()
        esPS2.close()
        esA.close()
        psTR.close()

        esC = ExitStack()
        x1sb = mksb(esC, "x1sb", [128, 8 * 2048], F32)
        esC3 = ExitStack()
        wob = mksb(esC3, "wob", [128, 16 * 2048], BF16)
        otld = [mksb(esC3, f"otld{i}", [128, 16 * 256], BF16) for i in range(2)]
        xo = [mksb(esC3, f"xo{i}", [128, 2048], F32) for i in range(2)]
        ps3st = ExitStack()
        ps3 = mkps(ps3st, "ps3", [128, 4096])

        tok_attn_done = ("eT", cnt["eT"])
        waits("gpsimd", tok_attn_done)
        tok_wob = dma("gpsimd", wob[:],
                      bass.AP(wo.tensor, 0, [[2048, 128], [128 * 2048, 16], [1, 2048]]))
        waits("tensor", exp_tok[0], exp_tok[1], ot_evac_t[0], c_evac)

        ot_ld_used = [None, None]
        xo_used = [None, None]
        ps3_free = [None] * 8
        for jp in range(4):
            b = jp % 2
            waits("sync", tok_ot_final, ot_ld_used[b])
            tol = None
            for hd in range(H):
                tol = dma("sync", otld[b][:, hd * 256:(hd + 1) * 256],
                          ots[hd, :, jp * 256:(jp + 1) * 256])
            for jj in range(2):
                j = jp * 2 + jj
                waits("sync", xo_used[jj])
                txo = dma("sync", xo[jj][:], xb[2 * j * 128:(2 * j + 1) * 128, :])
                lastmm = [None] * 4
                waits("tensor", tok_wob, tol)
                for cc in range(4):
                    waits("tensor", ps3_free[jj * 4 + cc])
                for hd in range(H):
                    for cc in range(4):
                        sl = jj * 4 + cc
                        lastmm[cc] = op("tensor", lambda E, b=b, hd=hd, cc=cc, jj=jj, sl=sl: E.matmul(
                            ps3[:, sl * 512:(sl + 1) * 512],
                            otld[b][:, hd * 256 + jj * 128: hd * 256 + jj * 128 + 128],
                            wob[:, hd * 2048 + cc * 512: hd * 2048 + (cc + 1) * 512],
                            start=(hd == 0), stop=(hd == 15)),
                            "eT" if hd == 15 else None)
                ot_ld_used[b] = lastmm[3]
                for cc in range(4):
                    sl = jj * 4 + cc
                    waits("vector", lastmm[cc], txo)
                    tev = opd("vector", lambda E, j=j, cc=cc, sl=sl, jj=jj: E.tensor_add(
                        x1sb[:, j * 2048 + cc * 512: j * 2048 + (cc + 1) * 512],
                        ps3[:, sl * 512:(sl + 1) * 512],
                        xo[jj][:, cc * 512:(cc + 1) * 512]), "eV")
                    ps3_free[sl] = tev
                xo_used[jj] = ps3_free[jj * 4 + 3]

        ps3_toks = list(ps3_free)
        ps3st.close()
        esC3.close()

        s_tp = mksb(root, "s_tp", [128, 2048], BF16, side="right")
        esD = ExitStack()
        ln2xt = mksb(esD, "ln2xt", [128, 16 * 1024], BF16, side="right")
        s_t = mksb(esD, "s_t", [128, 2 * HFF], BF16, side="right")
        esP4 = ExitStack()
        l2sts = [mksb(esP4, f"l2st{i}", [128, 2048], BF16) for i in range(2)]
        st2_done = [None, None]
        tok_ln2s = None
        prev_tv = None
        for j in range(NJ):
            b = j % 2
            xv = x1sb[:, j * 2048:(j + 1) * 2048]
            for sl in range(8):
                waits("scalar", ps3_free[sl])
            waits("scalar", prev_tv)
            t1 = op("scalar", lambda E, xv=xv: E.activation(
                junkb[:], xv, AF.Identity, accum_out=stats[:, 0:1]), "eS")
            t2 = op("scalar", lambda E, xv=xv: E.activation(
                junkb[:], xv, AF.Square, accum_out=stats[:, 1:2]), "eS")
            waits("vector", t2)
            op("vector", lambda E: E.tensor_scalar(
                stats[:, 2:3], stats[:, 0:1], 1.0 / 2048.0, None, ALU.mult))
            op("vector", lambda E: E.tensor_scalar(
                stats[:, 3:4], stats[:, 1:2], 1.0 / 2048.0, None, ALU.mult))
            op("vector", lambda E: E.tensor_mul(
                stats[:, 4:5], stats[:, 2:3], stats[:, 2:3]))
            bar("vector")
            op("vector", lambda E: E.tensor_sub(
                stats[:, 5:6], stats[:, 3:4], stats[:, 4:5]))
            bar("vector")
            tv = op("vector", lambda E: E.tensor_scalar(
                stats[:, 5:6], stats[:, 5:6], 1.0, EPS_LN, ALU.mult, ALU.add), "eV")
            prev_tv = tv
            waits("scalar", tv)
            tq = op("scalar", lambda E: E.activation(
                stats[:, 6:7], stats[:, 5:6], AF.Sqrt), "eS")
            waits("vector", tq, st2_done[b])
            op("vector", lambda E: E.reciprocal(stats[:, 7:8], stats[:, 6:7]))
            bar("vector")
            op("vector", lambda E: E.scalar_tensor_tensor(
                stats[:, 8:9], stats[:, 2:3], -1.0, stats[:, 7:8],
                ALU.mult, ALU.mult))
            bar("vector")
            ta = opd("vector", lambda E, b=b, xv=xv: E.tensor_scalar(
                l2sts[b][:], xv, stats[:, 7:8], stats[:, 8:9],
                ALU.mult, ALU.add), "eV")
            waits("sync", ta)
            tok_ln2s = dma("sync", ln2s[j * 128:(j + 1) * 128, :], l2sts[b][:])
            st2_done[b] = tok_ln2s
            dma("sync", x1s[j * 128:(j + 1) * 128, :], xv)
        tok_x1_spill = ("qS", cnt["qS"])
        esP4.close()

        waits("scalar", tok_ln2s)
        tok_ln2xt = None
        for dt in range(16):
            tok_ln2xt = dma("scalar", ln2xt[:, dt * 1024:(dt + 1) * 1024],
                            ln2s[:, dt * 128:(dt + 1) * 128], transpose=True, sem="qT")
        waits("scalar", tok_cp_all)
        tok_st_raw = dma("scalar", s_t[:], sscr, transpose=True, sem="qT")
        tok_stp_raw = dma("scalar", s_tp[:], sscrp, transpose=True, sem="qT")
        waits("vector", tok_st_raw)
        tok_st = opd("vector", lambda E: E.tensor_scalar(
            s_t[0:32, :], s_t[0:32, :], 0.5, None, ALU.mult), "eV")
        waits("vector", tok_stp_raw)
        tok_stp = opd("vector", lambda E: E.tensor_scalar(
            s_tp[:], s_tp[:], 0.5, None, ALU.mult), "eV")
        waits("sync", tok_st)
        tok_sct = dma("sync", sscrt, s_t[0:32, :])

        esC.close()

        esE = ExitStack()
        ps5st = ExitStack()
        ps5 = mkps(ps5st, "ps5", [128, 4096])
        seb = [mksb(esE, f"seb{i}", [128, 8192], BF16) for i in range(2)]
        NQB = 3
        rq = [mksb(esE, f"rq{i}", [128, 512], BF16) for i in range(NQB)]
        qa = [mksb(esE, f"qa{i}", [128, 512], BF16) for i in range(NQB)]
        qt = [mksb(esE, f"qt{i}", [128, 512], BF16) for i in range(NQB)]
        sgb = mksb(esE, "sgb", [128, 8 * 512], BF16)
        sub_ = mksb(esE, "sub", [128, 512], BF16)
        htile = mksb(esE, "htile", [128, 512], BF16)

        waits("scalar", tok_x1_spill, tok_cpw)
        waits("vector", tok_x1_spill)
        waits("gpsimd", tok_x1_spill)
        se_load = [None, None]
        se_cons = [None, None]
        rq_used = [None] * NQB
        qa_free = [None] * NQB
        qt_free = [None] * NQB
        bank_mm = [None] * NJ
        bank_ev = [None] * NJ
        sg_used = [None] * NJ
        h_written = None
        htile_busy = None
        sub_busy = None
        last_q6 = None

        def quant_tile(i, rd_tok, sp):
            o = quant_off[0]
            waits("vector", rd_tok, se_load[sp], qa_free[i])
            t2 = op("vector", lambda E, i=i, sp=sp, o=o: E.tensor_tensor(
                qa[i][:], rq[i][:], seb[sp][:, o:o + 512], ALU.is_gt), "eV")
            waits("gpsimd", rd_tok, se_load[sp], qt_free[i])
            t3 = op("gpsimd", lambda E, i=i, sp=sp, o=o: E.tensor_tensor(
                qt[i][:], rq[i][:], seb[sp][:, o:o + 512], ALU.add), "eG")
            waits("vector", t3)
            t4 = op("vector", lambda E, i=i: E.tensor_scalar(
                qt[i][:], qt[i][:], 0.0, None, ALU.is_lt), "eV")
            waits("gpsimd", t2, t4)
            t5 = op("gpsimd", lambda E, i=i: E.tensor_tensor(
                qa[i][:], qa[i][:], qt[i][:], ALU.subtract), "eG")
            qt_free[i] = t5
            waits("vector", t5)
            t6 = opd("vector", lambda E, i=i, sp=sp, o=o: E.scalar_tensor_tensor(
                rq[i][:], qa[i][:], 2.0, seb[sp][:, o:o + 512],
                ALU.mult, ALU.mult), "eV")
            qa_free[i] = t6
            se_cons[sp] = t6
            return t6

        quant_off = [0]
        for hs in range(16):
            sp = hs % 2
            for gu_i in range(2):
                base_row = gu_i * HFF
                waits("scalar", tok_st, se_cons[sp])
                guard_se = se_cons[sp]

                def issue_rd(dt, guard_se=guard_se):
                    i = dt % NQB
                    off_src = (2 * dt) * (2 * HFF) + base_row + hs * 512
                    if dt % 2 == 0:
                        tse = dma("scalar", seb[sp][:, dt * 512:(dt + 1) * 512],
                                  bass.AP(s_t, off_src,
                                          [[2 * HFF, 2], [0, 64], [1, 512]]), sem="qW")
                    else:
                        waits("sync", tok_sct, guard_se)
                        tse = dma("sync", seb[sp][:, dt * 512:(dt + 1) * 512],
                                  bass.AP(sscrt.tensor, off_src,
                                          [[2 * HFF, 2], [0, 64], [1, 512]]), sem="qY")
                    waits("scalar", rq_used[i], qa_free[i])
                    trd = dma("scalar", rq[i][:],
                              wgubT[dt * 128:(dt + 1) * 128,
                                    base_row + hs * 512: base_row + (hs + 1) * 512],
                              sem="qW")
                    rd_tok_n[dt] = (trd, tse)

                rd_tok_n = [None] * 16
                issue_rd(0)
                issue_rd(1)
                for dt in range(16):
                    i = dt % NQB
                    if dt + 2 < 16:
                        issue_rd(dt + 2)
                    trd, tse = rd_tok_n[dt]
                    se_load[sp] = tse
                    quant_off[0] = dt * 512
                    tq6 = quant_tile(i, trd, sp)
                    if DEBUG and hs == 0 and gu_i == 1:
                        waits("sync", tq6)
                        dma("sync", dbg_wq[dt], rq[i][:])
                        tq6 = dma("sync", dbg_se[dt], seb[sp][:, dt * 512:(dt + 1) * 512])
                    waits("tensor", tq6, tok_ln2xt)
                    lm = None
                    for j in range(NJ):
                        if dt == 0:
                            waits("tensor", bank_ev[j])
                        lm = op("tensor", lambda E, i=i, dt=dt, j=j: E.matmul(
                            ps5[:, j * 512:(j + 1) * 512],
                            ln2xt[:, dt * 1024 + j * 128: dt * 1024 + j * 128 + 128],
                            rq[i][:],
                            start=(dt == 0), stop=(dt == 15)),
                            "eT" if (j == NJ - 1) else None)
                        if dt == 15:
                            bank_mm[j] = lm if j == NJ - 1 else None
                    rq_used[i] = lm
                    if dt == 15:
                        for j in range(NJ):
                            bank_mm[j] = lm
                if gu_i == 0:
                    if DEBUG and hs == 0:
                        for j in range(NJ):
                            waits("vector", bank_mm[j])
                            tgd = opd("vector", lambda E, j=j: E.tensor_copy(
                                junkb[:, 0:1024].rearrange("p (a b) -> p (a b)") if False else htile[:],
                                ps5[:, j * 512:(j + 1) * 512]), "eV")
                            waits("sync", tgd)
                            tgd2 = dma("sync", dbg_gu[j], htile[:])
                            waits("vector", tgd2)
                    for j in range(NJ):
                        waits("scalar", bank_mm[j], sg_used[j])
                        bank_ev[j] = op("scalar", lambda E, j=j: E.activation(
                            sgb[:, j * 512:(j + 1) * 512],
                            ps5[:, j * 512:(j + 1) * 512], AF.Silu), "eS")
                    sil_drain = op("scalar", lambda E: E.drain(), "eS")
                    bank_ev = [sil_drain] * NJ
                    if DEBUG and hs == 0:
                        waits("sync", sil_drain)
                        for j in range(NJ):
                            dsg = dma("sync", dbg_sgb[j], sgb[:, j * 512:(j + 1) * 512])
                        waits("scalar", dsg)
                else:
                    for j in range(NJ):
                        waits("scalar", bank_mm[j], sub_busy)
                        tcp = opd("scalar", lambda E, j=j: E.activation(
                            sub_[:], ps5[:, j * 512:(j + 1) * 512],
                            AF.Identity), "eS")
                        bank_ev[j] = tcp
                        if DEBUG and hs == 0:
                            waits("sync", tcp)
                            tdu = dma("sync", dbg_up[j], sub_[:])
                            waits("scalar", tdu)
                        waits("vector", tcp, htile_busy, h_written)
                        tmul = opd("vector", lambda E, j=j: E.tensor_mul(
                            htile[:], sgb[:, j * 512:(j + 1) * 512], sub_[:]), "eV")
                        sub_busy = tmul
                        sg_used[j] = tmul
                        waits("scalar", tmul)
                        tsq = op("scalar", lambda E, j=j, hs=hs: E.activation(
                            junkb[:, 0:512], htile[:], AF.Square,
                            accum_out=ssqp[:, j * 16 + hs: j * 16 + hs + 1]), "eS")
                        htile_busy = tsq
                        waits("sync", tmul)
                        if DEBUG and hs == 0:
                            tdh = dma("sync", dbg_ht[j], htile[:])
                            htile_busy = ("qS", tdh[1])
                        h_written = dma("sync",
                                        hhb[j * 128:(j + 1) * 128,
                                            hs * 512:(hs + 1) * 512], htile[:])
        tok_h_all = ("qS", cnt["qS"])
        tok_sq_all = ("eS", cnt["eS"])
        tok_gu_done = ("eT", cnt["eT"])

        waits("vector", tok_sq_all)
        s3v = ssqp[:].rearrange("p (j c) -> p j c", c=16)
        op("vector", lambda E, s3v=s3v: E.tensor_reduce(
            stats[:, 16:24], s3v, AX.X, ALU.add))
        bar("vector")
        tv6 = op("vector", lambda E: E.tensor_scalar(
            stats[:, 16:24], stats[:, 16:24], 1.0 / float(HFF), EPS_RMS,
            ALU.mult, ALU.add), "eV")
        waits("scalar", tv6)
        tq6_ = op("scalar", lambda E: E.activation(
            invj[:], stats[:, 16:24], AF.Sqrt), "eS")
        waits("vector", tq6_)
        tinv = op("vector", lambda E: E.reciprocal(invj[:], invj[:]), "eV")

        ps5st.close()
        esE.close()
        esD_keep_stp = None
        esD.close()

        esF = ExitStack()
        ps7st = ExitStack()
        ps7 = [mkps(ps7st, "ps7a", [128, 2048]), mkps(ps7st, "ps7b", [128, 2048])]
        acc = mksb(esF, "acc", [128, 8 * 2048], F32)
        wqc = mksb(esF, "wqc", [128, 8 * 2048], BF16)
        htc = mksb(esF, "htc", [128, 8 * 1024], BF16)
        sep = [mksb(esF, f"sep{i}", [128, 2048], BF16) for i in range(2)]
        x1b = [mksb(esF, f"x1b{i}", [128, 2048], F32) for i in range(2)]
        ysb = [mksb(esF, f"ysb{i}", [128, 2048], F32) for i in range(2)]
        PQB = 2
        prq = [mksb(esF, f"prq{i}", [128, 2048], BF16) for i in range(PQB)]
        pqa = [mksb(esF, f"pqa{i}", [128, 2048], BF16) for i in range(PQB)]
        pqt = [mksb(esF, f"pqt{i}", [128, 2048], BF16) for i in range(PQB)]

        waits("scalar", tok_gu_done, tok_sq_all)
        waits("vector", tok_gu_done)
        waits("gpsimd", tok_gu_done)
        waits("tensor", tok_sq_all)
        sep_load = [None, None]
        sep_cons = [None, None]
        pqa_free = [None] * PQB
        pqt_free = [None] * PQB
        wqc_used = None
        htc_used = None
        acc_done = [None] * NJ
        ps7_free = [None, None]

        def quant_tile_p(i, rd_tok, dst_off):
            waits("vector", rd_tok, sep_load[i], pqa_free[i])
            t2 = op("vector", lambda E, i=i: E.tensor_tensor(
                pqa[i][:], prq[i][:], sep[i][:], ALU.is_gt), "eV")
            waits("gpsimd", rd_tok, sep_load[i], pqt_free[i])
            t3 = op("gpsimd", lambda E, i=i: E.tensor_tensor(
                pqt[i][:], prq[i][:], sep[i][:], ALU.add), "eG")
            waits("vector", t3)
            t4 = op("vector", lambda E, i=i: E.tensor_scalar(
                pqt[i][:], pqt[i][:], 0.0, None, ALU.is_lt), "eV")
            waits("gpsimd", t2, t4)
            t5 = op("gpsimd", lambda E, i=i: E.tensor_tensor(
                pqa[i][:], pqa[i][:], pqt[i][:], ALU.subtract), "eG")
            pqt_free[i] = t5
            waits("vector", t5, wqc_used)
            t6 = opd("vector", lambda E, i=i, o=dst_off: E.scalar_tensor_tensor(
                wqc[:, o:o + 2048], pqa[i][:], 2.0, sep[i][:],
                ALU.mult, ALU.mult), "eV")
            pqa_free[i] = t6
            sep_cons[i] = t6
            return t6

        for hcg in range(8):
            wq_toks = []
            ht_toks = []
            rd_p = {}
            for i in range(2):
                hc = hcg * 8 + i
                qb = i % PQB
                waits("scalar", tok_stp, pqa_free[qb], sep_cons[qb])
                trd = dma("scalar", prq[qb][:],
                          wprbT[hc * 128:(hc + 1) * 128, :], sem="qW")
                tsep = dma("scalar", sep[qb][:],
                           bass.AP(s_tp, 2 * hc * 2048,
                                   [[2048, 2], [0, 64], [1, 2048]]), sem="qW")
                rd_p[i] = (trd, tsep)
            for i in range(8):
                hc = hcg * 8 + i
                qb = i % PQB
                trd, tsep = rd_p[i]
                sep_load[qb] = tsep
                wq_toks.append(quant_tile_p(qb, trd, i * 2048))
                if DEBUG and hcg == 0 and i < 2:
                    waits("sync", wq_toks[-1])
                    dma("sync", dbg_prq[i], prq[qb][:])
                    dma("sync", dbg_sep[i], sep[qb][:])
                    tqd = dma("sync", dbg_pqa[i], pqa[qb][:])
                    pqa_free[qb] = ("qS", tqd[1])
                    sep_cons[qb] = ("qS", tqd[1])
                if i + 2 < 8:
                    hc2 = hcg * 8 + i + 2
                    qb2 = (i + 2) % PQB
                    waits("scalar", tok_stp, pqa_free[qb2], sep_cons[qb2])
                    trd2 = dma("scalar", prq[qb2][:],
                              wprbT[hc2 * 128:(hc2 + 1) * 128, :], sem="qW")
                    tsep2 = dma("scalar", sep[qb2][:],
                               bass.AP(s_tp, 2 * hc2 * 2048,
                                       [[2048, 2], [0, 64], [1, 2048]]), sem="qW")
                    rd_p[i + 2] = (trd2, tsep2)
                waits("scalar", tok_h_all, htc_used)
                ht_toks.append(dma("scalar", htc[:, i * 1024:(i + 1) * 1024],
                                   hhb[:, hc * 128:(hc + 1) * 128], transpose=True, sem="qW"))
            for tki in wq_toks:
                waits("tensor", tki)
            waits("tensor", ht_toks[-1])
            lastmm = None
            for j in range(NJ):
                par = j % 2
                waits("tensor", ps7_free[par])
                lm = None
                for i in range(8):
                    for dc in range(4):
                        lm = op("tensor", lambda E, i=i, dc=dc, j=j, par=par: E.matmul(
                            ps7[par][:, dc * 512:(dc + 1) * 512],
                            htc[:, i * 1024 + j * 128: i * 1024 + j * 128 + 128],
                            wqc[:, i * 2048 + dc * 512: i * 2048 + (dc + 1) * 512],
                            start=(i == 0), stop=(i == 7)),
                            "eT" if (i == 7 and dc == 3) else None)
                lastmm = lm
                waits("vector", lm, acc_done[j])
                if hcg == 0:
                    tev = op("vector", lambda E, j=j, par=par: E.tensor_copy(
                        acc[:, j * 2048:(j + 1) * 2048], ps7[par][:]), "eV")
                else:
                    tev = op("vector", lambda E, j=j, par=par: E.tensor_add(
                        acc[:, j * 2048:(j + 1) * 2048],
                        acc[:, j * 2048:(j + 1) * 2048], ps7[par][:]), "eV")
                acc_done[j] = tev
                ps7_free[par] = tev
            wqc_used = lastmm
            htc_used = lastmm
            if DEBUG and hcg == 0:
                waits("sync", lastmm)
                dma("sync", dbg_wqc, wqc[:])
                dma("sync", dbg_htc, htc[:])
                for j in range(NJ):
                    waits("sync", acc_done[j])
                tac0 = dma("sync", dbg_acc0, acc[:])
                wqc_used = ("qS", tac0[1])
                htc_used = ("qS", tac0[1])

        if DEBUG:
            for j in range(NJ):
                waits("sync", acc_done[j])
            dma("sync", dbg_acc, acc[:])

        for t_ in acc_done:
            waits("sync", t_)
        y_tok = [None, None]
        for j in range(NJ):
            b = j % 2
            waits("sync", tok_x1_spill, y_tok[b])
            tx1 = dma("sync", x1b[b][:], x1s[j * 128:(j + 1) * 128, :])
            waits("vector", tx1, acc_done[j], tinv, y_tok[b])
            tfin = opd("vector", lambda E, j=j, b=b: E.scalar_tensor_tensor(
                ysb[b][:], acc[:, j * 2048:(j + 1) * 2048],
                invj[:, j:j + 1], x1b[b][:], ALU.mult, ALU.add), "eV")
            waits("sync", tfin)
            y_tok[b] = dma("sync", y[j * 128:(j + 1) * 128, :], ysb[b][:])

        if DEBUG:
            waits("gpsimd", ("qS", cnt["qS"]))
            dma("gpsimd", dbg_ln1, ln1s)
            dma("gpsimd", dbg_vts, vts)
            dma("gpsimd", dbg_ots, ots)
            dma("gpsimd", dbg_x1, x1s)
            dma("gpsimd", dbg_ln2, ln2s)
            dma("gpsimd", dbg_h, hhb)
            dma("gpsimd", dbg_ss, sscr)
            dma("gpsimd", dbg_wgub, wgubT)
            waits("sync", tinv)
            dma("sync", dbg_inv, invj[:])
            emit("gpsimd", lambda E, v=cnt["qG"]: E.wait_ge(semh["qG"], v))
        for nm_ in ("qS", "qA", "qG", "qM", "qW", "qT", "qY"):
            emit("sync", lambda E, nm_=nm_, v=cnt[nm_]: E.wait_ge(semh[nm_], v))

        ps7st.close()
        esF.close()

        with nc.Block() as block:
            @block.sync
            def _(E):
                for f in plan["sync"]:
                    f(E)

            @block.scalar
            def _(E):
                for f in plan["scalar"]:
                    f(E)

            @block.vector
            def _(E):
                for f in plan["vector"]:
                    f(E)

            @block.tensor
            def _(E):
                for f in plan["tensor"]:
                    f(E)

            @block.gpsimd
            def _(E):
                for f in plan["gpsimd"]:
                    f(E)

    return nc


def _host_masks(half):
    gm = np.zeros((NJ, 128, NB), np.float32)
    cm = np.zeros((NJ, 128, NB), np.float32)
    kb_abs = np.zeros(NB, np.int64)
    for u in range(16):
        a = (u ^ 1) if half == 1 else u
        kb_abs[2 * u] = 2 * a
        kb_abs[2 * u + 1] = 2 * a + 1
    for j in range(NJ):
        T = 2 * j + half
        for i in range(128):
            qb = (T * 128 + i) // 64
            gm[j, i][kb_abs == qb] = 1e30
            gm[j, i][kb_abs > qb] = -1e30
            cm[j, i] = (kb_abs <= qb).astype(np.float32)
    return gm, cm


def kernel(L0_attn_hidden_states, L0_attn_Wq, L0_attn_Wk, L0_attn_Wv,
           L0_attn_Wo, L0_ffn_w_gate_up, L0_ffn_w_proj):
    x = np.ascontiguousarray(np.asarray(L0_attn_hidden_states, dtype=np.float32))
    Wq = np.ascontiguousarray(np.asarray(L0_attn_Wq, dtype=np.float32))
    Wk = np.ascontiguousarray(np.asarray(L0_attn_Wk, dtype=np.float32))
    Wv = np.ascontiguousarray(np.asarray(L0_attn_Wv, dtype=np.float32))
    Wo = np.ascontiguousarray(np.asarray(L0_attn_Wo, dtype=np.float32))
    wgu_ = np.ascontiguousarray(np.asarray(L0_ffn_w_gate_up, dtype=np.float32))
    wpr_ = np.ascontiguousarray(np.asarray(L0_ffn_w_proj, dtype=np.float32))

    i128c = np.eye(128).astype(bfdt)
    idfc = np.eye(128).astype(np.float32)
    tric = np.where(np.arange(128)[:, None] > np.arange(128)[None, :],
                    NEG, 0.0).astype(bfdt)
    onesc = np.ones((128, 1)).astype(bfdt)

    nc = build_nc()
    in_maps = []
    for c in range(8):
        b, half = c // 2, c % 2
        xbv = x[b]
        if half == 1:
            xbv = xbv.reshape(16, 128, DM)[np.arange(16) ^ 1].reshape(S, DM)
        gm, cm = _host_masks(half)
        in_maps.append({
            "xb": np.ascontiguousarray(xbv),
            "wq": Wq, "wk": Wk, "wv": Wv, "wo": Wo,
            "wgu": wgu_, "wpr": wpr_,
            "gm": gm, "cmask": cm,
            "i128": i128c, "idf": idfc, "tri": tric, "ones1": onesc,
        })
    res = run_bass_kernel_spmd(nc, in_maps, core_ids=list(range(8)))

    out = np.empty((4, S, DM), np.float32)
    for c in range(8):
        b, half = c // 2, c % 2
        yc = np.asarray(res.results[c]["y"])
        if yc.dtype != np.float32:
            yc = yc.view(np.float32)
        yc = yc.reshape(NJ, 128, DM)
        for j in range(NJ):
            out[b, (2 * j + half) * 128:(2 * j + half + 1) * 128, :] = yc[j]
    return out


if __name__ == "__main__":
    import reference as _r
    ins = _r.setup_inputs()
    outs = kernel(**{k: np.asarray(v) for k, v in ins.items()})
    ref = np.asarray(_r.reference(**ins))
    err = np.abs(outs - ref)
    print("relL2:", float(np.sqrt((err ** 2).mean() / (ref ** 2).mean())),
          "max:", float(err.max()))


# version 05
# speedup vs baseline: 1.2526x; 1.2526x over previous
import os
import numpy as np
import ml_dtypes
import concourse.bass as bass
import concourse.mybir as mybir
from concourse.bass_utils import run_bass_kernel_spmd
from contextlib import ExitStack

F32 = mybir.dt.float32
BF16 = mybir.dt.bfloat16
AF = mybir.ActivationFunctionType
ALU = mybir.AluOpType
AX = mybir.AxisListType

H, D, S, DM = 16, 128, 2048, 2048
NB, BLK = 32, 64
HFF = 8192
NJ = 8
SM = float(D) ** -0.5
NEG = -1e31
EPS_LN = 1e-5
EPS_RMS = 1.1920929e-07
bfdt = ml_dtypes.bfloat16

ENGS = ("sync", "scalar", "vector", "tensor", "gpsimd")
RING = {"sync": "qS", "scalar": "qA", "gpsimd": "qG"}


DEBUG = os.environ.get("KDBG", "") == "1"


def build_nc():
    nc = bass.Bass("TRN2", target_bir_lowering=False, debug=False)

    xb = nc.dram_tensor("xb", [S, DM], F32, kind="ExternalInput").ap()
    wq = nc.dram_tensor("wq", [DM, DM], F32, kind="ExternalInput").ap()
    wk = nc.dram_tensor("wk", [DM, DM], F32, kind="ExternalInput").ap()
    wv = nc.dram_tensor("wv", [DM, DM], F32, kind="ExternalInput").ap()
    wo = nc.dram_tensor("wo", [DM, DM], F32, kind="ExternalInput").ap()
    wgu = nc.dram_tensor("wgu", [2 * HFF, DM], F32, kind="ExternalInput").ap()
    wpr = nc.dram_tensor("wpr", [DM, HFF], F32, kind="ExternalInput").ap()
    gmi = nc.dram_tensor("gm", [NJ, 128, NB], F32, kind="ExternalInput").ap()
    cmi = nc.dram_tensor("cmask", [NJ, 128, NB], F32, kind="ExternalInput").ap()
    i128 = nc.dram_tensor("i128", [128, 128], BF16, kind="ExternalInput").ap()
    idf = nc.dram_tensor("idf", [128, 128], F32, kind="ExternalInput").ap()
    tri = nc.dram_tensor("tri", [128, 128], BF16, kind="ExternalInput").ap()
    ones1 = nc.dram_tensor("ones1", [128, 1], BF16, kind="ExternalInput").ap()
    y = nc.dram_tensor("y", [NJ * 128, DM], F32, kind="ExternalOutput").ap()

    ln1s = nc.dram_tensor("ln1s", [S, DM], BF16, kind="Internal").ap()
    vts = nc.dram_tensor("vts", [DM, S], BF16, kind="Internal").ap()
    ots = nc.dram_tensor("ots", [H, 128, 1024], BF16, kind="Internal").ap()
    x1s = nc.dram_tensor("x1s", [1024, DM], F32, kind="Internal").ap()
    ln2s = nc.dram_tensor("ln2s", [1024, DM], BF16, kind="Internal").ap()
    wgubT = nc.dram_tensor("wgubT", [DM, 2 * HFF], BF16, kind="Internal").ap()
    wprbT = nc.dram_tensor("wprbT", [HFF, DM], BF16, kind="Internal").ap()
    sscr = nc.dram_tensor("sscr", [2 * HFF, 128], BF16, kind="Internal").ap()
    sscrp = nc.dram_tensor("sscrp", [DM, 128], BF16, kind="Internal").ap()
    sscrt = nc.dram_tensor("sscrt", [32, 2 * HFF], BF16, kind="Internal").ap()
    hhb = nc.dram_tensor("hhb", [1024, HFF], BF16, kind="Internal").ap()
    if DEBUG:
        dbg_ln1 = nc.dram_tensor("dbg_ln1", [S, DM], BF16, kind="ExternalOutput").ap()
        dbg_vts = nc.dram_tensor("dbg_vts", [DM, S], BF16, kind="ExternalOutput").ap()
        dbg_ots = nc.dram_tensor("dbg_ots", [H, 128, 1024], BF16, kind="ExternalOutput").ap()
        dbg_x1 = nc.dram_tensor("dbg_x1", [1024, DM], F32, kind="ExternalOutput").ap()
        dbg_ln2 = nc.dram_tensor("dbg_ln2", [1024, DM], BF16, kind="ExternalOutput").ap()
        dbg_h = nc.dram_tensor("dbg_h", [1024, HFF], BF16, kind="ExternalOutput").ap()
        dbg_ss = nc.dram_tensor("dbg_ss", [2 * HFF, 128], BF16, kind="ExternalOutput").ap()
        dbg_wgub = nc.dram_tensor("dbg_wgub", [DM, 2 * HFF], BF16, kind="ExternalOutput").ap()
        dbg_sig = nc.dram_tensor("dbg_sig", [H, 1024], F32, kind="ExternalOutput").ap()
        dbg_gb = nc.dram_tensor("dbg_gb", [H, 32, 1024], BF16, kind="ExternalOutput").ap()
        dbg_inv = nc.dram_tensor("dbg_inv", [128, 8], F32, kind="ExternalOutput").ap()
        dbg_k = nc.dram_tensor("dbg_k", [128, 2048], BF16, kind="ExternalOutput").ap()
        dbg_kg = nc.dram_tensor("dbg_kg", [128, 32], BF16, kind="ExternalOutput").ap()
        dbg_q = nc.dram_tensor("dbg_q", [128, 1024], BF16, kind="ExternalOutput").ap()
        dbg_gsb = nc.dram_tensor("dbg_gsb", [128, 256], F32, kind="ExternalOutput").ap()
        dbg_th8 = nc.dram_tensor("dbg_th8", [128, 64], F32, kind="ExternalOutput").ap()
        dbg_sigr = nc.dram_tensor("dbg_sigr", [H, 1024], F32, kind="ExternalOutput").ap()
        dbg_pt = nc.dram_tensor("dbg_pt", [16, 128, 1024], BF16, kind="ExternalOutput").ap()
        dbg_me = nc.dram_tensor("dbg_me", [16, 128, 1024], BF16, kind="ExternalOutput").ap()
        dbg_pt2 = nc.dram_tensor("dbg_pt2", [16, 128, 1024], BF16, kind="ExternalOutput").ap()
        dbg_me2 = nc.dram_tensor("dbg_me2", [2, 128, 1024], BF16, kind="ExternalOutput").ap()
        dbg_wq = nc.dram_tensor("dbg_wq", [16, 128, 512], BF16, kind="ExternalOutput").ap()
        dbg_se = nc.dram_tensor("dbg_se", [16, 128, 512], BF16, kind="ExternalOutput").ap()
        dbg_gu = nc.dram_tensor("dbg_gu", [8, 128, 512], BF16, kind="ExternalOutput").ap()
        dbg_sgb = nc.dram_tensor("dbg_sgb", [8, 128, 512], BF16, kind="ExternalOutput").ap()
        dbg_up = nc.dram_tensor("dbg_up", [8, 128, 512], BF16, kind="ExternalOutput").ap()
        dbg_ht = nc.dram_tensor("dbg_ht", [8, 128, 512], BF16, kind="ExternalOutput").ap()
        dbg_wqc = nc.dram_tensor("dbg_wqc", [128, 8 * 2048], BF16, kind="ExternalOutput").ap()
        dbg_htc = nc.dram_tensor("dbg_htc", [128, 8 * 1024], BF16, kind="ExternalOutput").ap()
        dbg_acc = nc.dram_tensor("dbg_acc", [128, 8 * 2048], F32, kind="ExternalOutput").ap()
        dbg_acc0 = nc.dram_tensor("dbg_acc0", [128, 8 * 2048], F32, kind="ExternalOutput").ap()
        dbg_prq = nc.dram_tensor("dbg_prq", [2, 128, 2048], BF16, kind="ExternalOutput").ap()
        dbg_sep = nc.dram_tensor("dbg_sep", [2, 128, 2048], BF16, kind="ExternalOutput").ap()
        dbg_pqa = nc.dram_tensor("dbg_pqa", [2, 128, 2048], BF16, kind="ExternalOutput").ap()

    plan = {e: [] for e in ENGS}
    cnt = {"qS": 0, "qA": 0, "qG": 0, "qM": 0, "qW": 0, "qT": 0, "qY": 0,
           "eT": 0, "eV": 0, "eS": 0, "eG": 0}
    RINGSEMS = ("qS", "qA", "qG", "qM", "qW", "qT", "qY")
    last_wait = {e: {k: 0 for k in cnt} for e in ENGS}
    semh = {}

    def emit(eng, fn):
        plan[eng].append(fn)

    def wait(eng, tok):
        if tok is None:
            return
        s, v = tok
        if s in RINGSEMS:
            v = cnt[s]
        if v <= 0 or last_wait[eng][s] >= v:
            return
        last_wait[eng][s] = v
        emit(eng, lambda E, s=s, v=v: E.wait_ge(semh[s], v))

    def waits(eng, *toks):
        for t in toks:
            wait(eng, t)

    def dma(eng, out, in_, transpose=False, sem=None):
        ring = sem if sem is not None else RING[eng]
        cnt[ring] += 16
        v = cnt[ring]

        def f(E, out=out, in_=in_, tr=transpose, s=ring):
            E.dma_start(out=out, in_=in_, transpose=tr).then_inc(semh[s], 16)
        emit(eng, f)
        return (ring, v)

    def bar(eng):
        emit(eng, lambda E: E.drain())

    def opd(eng, fn, sem):
        op(eng, fn)
        return op(eng, lambda E: E.drain(), sem)

    def op(eng, fn, sem=None):
        tok = None
        if sem is not None:
            cnt[sem] += 1
            tok = (sem, cnt[sem])

        def f(E, fn=fn, tok=tok):
            ins = fn(E)
            if tok is not None:
                ins.then_inc(semh[tok[0]], 1)
        emit(eng, f)
        return tok

    root = ExitStack()

    def mksb(st, name, shape, dtype, side=None):
        if side is None:
            return st.enter_context(nc.sbuf_tensor(name, shape, dtype))
        return st.enter_context(nc.sbuf_tensor(name, shape, dtype, side=side))

    def mkps(st, name, shape):
        return st.enter_context(nc.psum_tensor(name, shape, F32))

    with root:
        for nm in ("qS", "qA", "qG", "qM", "qW", "qT", "qY",
                   "eT", "eV", "eS", "eG"):
            semh[nm] = root.enter_context(nc.semaphore(nm))

        c_i128 = mksb(root, "c_i128", [128, 128], BF16)
        c_idf = mksb(root, "c_idf", [128, 128], F32)
        c_tri = mksb(root, "c_tri", [128, 128], BF16)
        c_ones = mksb(root, "c_ones", [128, 1], BF16)
        gm_sb = mksb(root, "gm_sb", [128, NJ * 32], F32)
        cm_sb = mksb(root, "cm_sb", [128, NJ * 32], F32)
        stats = mksb(root, "stats", [128, 32], F32)
        junkb = mksb(root, "junkb", [128, 2048], BF16)
        ssqp = mksb(root, "ssqp", [128, 128], F32)
        invj = mksb(root, "invj", [128, 8], F32)

        dma("sync", c_i128[:], i128)
        dma("sync", c_idf[:], idf)
        dma("sync", c_tri[:], tri)
        dma("sync", c_ones[:], ones1)
        dma("sync", gm_sb[:], bass.AP(gmi.tensor, 0, [[32, 128], [128 * 32, NJ], [1, 32]]))
        tok_consts = dma("sync", cm_sb[:], bass.AP(cmi.tensor, 0, [[32, 128], [128 * 32, NJ], [1, 32]]))

        esA = ExitStack()
        ln1xt = mksb(esA, "ln1xt", [128, 16 * 2048], BF16)
        vall = mksb(esA, "vall", [128, 16 * 2048], BF16)
        cp_nat = [mksb(esA, f"cp_nat{i}", [128, 2048], BF16) for i in range(2)]
        cp_abs = mksb(esA, "cp_abs", [128, 2048], BF16)
        cp_sum = mksb(esA, "cp_sum", [128, 32], F32)
        cp_s = mksb(esA, "cp_s", [128, 32], BF16)
        cp_stage = [mksb(esA, "cp_stage0", [128, 1024], BF16)]
        cp_s3 = mksb(esA, "cp_s3", [128, 2048], BF16)
        cp_m = cp_abs
        cp_ng = mksb(esA, "cp_ng", [128, 2048], BF16)
        psTR = ExitStack()
        pstr = [psTR.enter_context(nc.psum_tensor(f"pstr{i}", [128, 1024], BF16))
                for i in range(2)]

        cp_busy = [None, None]
        cp_busy2 = [None, None]
        cp_abs_used = [None]
        cp_sum_used = [None]
        cp_s_used = [None]
        cp_pos = [0]
        cp_last_store = [None]
        tbank_free = [None, None]
        stage_free = [None]
        tbank_i = [0]
        stage_i = [0]

        def castpass_unit(u):
            buf = u % 2
            if u < 128:
                src = wgu[u * 128:(u + 1) * 128, :]
                sdst = sscr[u * 128:(u + 1) * 128, 0:32]
                trow = 2 * HFF
                toff0 = u * 128
            else:
                v0 = u - 128
                dt, pc = v0 // 4, v0 % 4
                src = wpr[dt * 128:(dt + 1) * 128, pc * 2048:(pc + 1) * 2048]
                sdst = sscrp[dt * 128:(dt + 1) * 128, pc * 32:(pc + 1) * 32]
                trow = DM
                toff0 = None
            waits("gpsimd", cp_busy[buf], cp_busy2[buf])
            tin = dma("gpsimd", cp_nat[buf][:], src)
            waits("scalar", tin, cp_abs_used[0])
            ta = op("scalar", lambda E, b=buf: E.activation(
                cp_abs[:], cp_nat[b][:], AF.Abs, scale=2.0), "eS")
            waits("vector", ta, cp_sum_used[0])
            a3 = cp_abs[:].rearrange("p (g c) -> p g c", c=64)
            tr = op("vector", lambda E, a3=a3: E.tensor_reduce(
                cp_sum[:], a3, AX.X, ALU.add), "eV")
            waits("vector", tr, cp_s_used[0])
            bar("vector")
            tm = opd("vector", lambda E: E.tensor_scalar(
                cp_s[:], cp_sum[:], 1.0 / 128.0, 1e-8, ALU.mult, ALU.max), "eV")
            cp_sum_used[0] = tm
            waits("sync", tm)
            cp_s_used[0] = dma("sync", sdst, cp_s[:])
            cp_last_store[0] = cp_s_used[0]
            op("vector", lambda E: E.tensor_copy(
                bass.AP(cp_s3, 0, [[2048, 128], [1, 64], [64, 32]]),
                bass.AP(cp_s, 0, [[32, 128], [0, 64], [1, 32]])))
            for hh in range(2):
                op("vector", lambda E, b=buf, o=hh * 1024: E.tensor_scalar(
                    cp_ng[:, o:o + 1024], cp_nat[b][:, o:o + 1024],
                    0.0, None, ALU.is_lt))
            for hh in range(2):
                op("vector", lambda E, o=hh * 1024: E.tensor_tensor(
                    cp_m[:, o:o + 1024], cp_abs[:, o:o + 1024],
                    cp_s3[:, o:o + 1024], ALU.is_gt))
            for hh in range(2):
                op("vector", lambda E, o=hh * 1024: E.tensor_scalar(
                    cp_ng[:, o:o + 1024], cp_ng[:, o:o + 1024],
                    -2.0, 1.0, ALU.mult, ALU.add))
            for hh in range(2):
                op("vector", lambda E, o=hh * 1024: E.tensor_tensor(
                    cp_m[:, o:o + 1024], cp_m[:, o:o + 1024],
                    cp_s3[:, o:o + 1024], ALU.mult))
            tq = None
            for hh in range(2):
                tq = op("vector", lambda E, b=buf, o=hh * 1024: E.tensor_tensor(
                    cp_nat[b][:, o:o + 1024], cp_m[:, o:o + 1024],
                    cp_ng[:, o:o + 1024], ALU.mult), "eV" if hh else None)
            tq = op("vector", lambda E: E.drain(), "eV")
            cp_abs_used[0] = tq
            ttr = None
            for halfu in range(2):
                bk = tbank_i[0]
                tbank_i[0] ^= 1
                waits("tensor", tok_consts, tq, tbank_free[bk])
                for t8 in range(8):
                    t = halfu * 8 + t8
                    ttr = op("tensor", lambda E, bk=bk, t8=t8, t=t, b=buf: E.transpose(
                        pstr[bk][:, t8 * 128:(t8 + 1) * 128],
                        cp_nat[b][:, t * 128:(t + 1) * 128], c_i128[:]),
                        "eT" if t8 == 7 else None)
                sg = 0
                waits("scalar", ttr, stage_free[sg])
                tev = opd("scalar", lambda E, sg=sg, bk=bk: E.activation(
                    cp_stage[sg][:], pstr[bk][:], AF.Identity), "eS")
                tbank_free[bk] = tev
                waits("gpsimd", tev)
                if u < 128:
                    doff = halfu * 1024 * trow + u * 128
                    ddst = bass.AP(wgubT.tensor, doff,
                                   [[trow, 128], [128 * trow, 8], [1, 128]])
                else:
                    doff = (pc * 2048 + halfu * 1024) * trow + dt * 128
                    ddst = bass.AP(wprbT.tensor, doff,
                                   [[trow, 128], [128 * trow, 8], [1, 128]])
                tst = dma("gpsimd", ddst,
                          bass.AP(cp_stage[sg], 0, [[1024, 128], [128, 8], [1, 128]]))
                stage_free[sg] = tst
            cp_busy[buf] = ta
            cp_busy2[buf] = ttr

        def castpass_step(n):
            for _ in range(n):
                if cp_pos[0] >= 192:
                    return
                castpass_unit(cp_pos[0])
                cp_pos[0] += 1

        esP1 = ExitStack()
        xts = [mksb(esP1, f"xt{i}", [128, 2048], F32) for i in range(2)]
        lsts = [mksb(esP1, f"lst{i}", [128, 2048], BF16) for i in range(2)]
        ap_done = [None, None]
        st_done = [None, None]
        prev_tv = None
        for tc in range(16):
            b = tc % 2
            waits("sync", ap_done[b])
            tl = dma("sync", xts[b][:], xb[tc * 128:(tc + 1) * 128, :])
            waits("scalar", tl, prev_tv)
            t1 = op("scalar", lambda E, b=b: E.activation(
                junkb[:], xts[b][:], AF.Identity, accum_out=stats[:, 0:1]), "eS")
            t2 = op("scalar", lambda E, b=b: E.activation(
                junkb[:], xts[b][:], AF.Square, accum_out=stats[:, 1:2]), "eS")
            waits("vector", t2)
            op("vector", lambda E: E.tensor_scalar(
                stats[:, 2:3], stats[:, 0:1], 1.0 / 2048.0, None, ALU.mult))
            op("vector", lambda E: E.tensor_scalar(
                stats[:, 3:4], stats[:, 1:2], 1.0 / 2048.0, None, ALU.mult))
            op("vector", lambda E: E.tensor_mul(
                stats[:, 4:5], stats[:, 2:3], stats[:, 2:3]))
            bar("vector")
            op("vector", lambda E: E.tensor_sub(
                stats[:, 5:6], stats[:, 3:4], stats[:, 4:5]))
            bar("vector")
            tv = op("vector", lambda E: E.tensor_scalar(
                stats[:, 5:6], stats[:, 5:6], 1.0, EPS_LN, ALU.mult, ALU.add), "eV")
            prev_tv = tv
            waits("scalar", tv)
            tq = op("scalar", lambda E: E.activation(
                stats[:, 6:7], stats[:, 5:6], AF.Sqrt), "eS")
            waits("vector", tq)
            op("vector", lambda E: E.reciprocal(stats[:, 7:8], stats[:, 6:7]))
            bar("vector")
            op("vector", lambda E: E.scalar_tensor_tensor(
                stats[:, 8:9], stats[:, 2:3], -1.0, stats[:, 7:8],
                ALU.mult, ALU.mult))
            bar("vector")
            waits("vector", st_done[b])
            ta = opd("vector", lambda E, b=b: E.tensor_scalar(
                lsts[b][:], xts[b][:], stats[:, 7:8], stats[:, 8:9],
                ALU.mult, ALU.add), "eV")
            ap_done[b] = ta
            waits("sync", ta)
            st_done[b] = dma("sync", ln1s[tc * 128:(tc + 1) * 128, :], lsts[b][:])
            if tc % 3 == 2:
                castpass_step(1)
        tok_ln1_stores = st_done[:]
        esP1.close()

        waits("scalar", tok_ln1_stores[0], tok_ln1_stores[1], tok_consts)
        tok_ln1xt = None
        for dt in range(16):
            tok_ln1xt = dma("scalar", ln1xt[:, dt * 2048:(dt + 1) * 2048],
                            ln1s[:, dt * 128:(dt + 1) * 128], transpose=True, sem="qT")

        esPS2 = ExitStack()
        psa = mkps(esPS2, "psa", [128, 512])
        psb = mkps(esPS2, "psb", [128, 512])
        psot = mkps(esPS2, "psot", [128, 1024])
        pssig = mkps(esPS2, "pssig", [1, 1024])

        esP2p = ExitStack()
        wvcs = [mksb(esP2p, f"wvc{i}", [128, 2048], BF16) for i in range(2)]
        vtsts = [mksb(esP2p, f"vtst{i}", [128, 1024], BF16) for i in range(2)]
        wv_used = [None, None]
        vt_stored = [None, None]
        ps_free = [None] * 4
        slices4 = [(psa, 0), (psb, 0), (psa, 0), (psb, 0)]
        vstore = None
        for vc in range(16):
            b = vc % 2
            waits("gpsimd", wv_used[b])
            tw = dma("gpsimd", wvcs[b][:],
                     bass.AP(wv.tensor, vc * 128,
                             [[2048, 128], [128 * 2048, 16], [1, 128]]))
            for cc in range(4):
                pst, off = slices4[cc]
                waits("tensor", tw, tok_ln1xt, ps_free[cc], ps_free[(cc + 2) % 4])
                lastmm = None
                for dt in range(16):
                    lastmm = op("tensor", lambda E, b=b, dt=dt, cc=cc, pst=pst, off=off: E.matmul(
                        pst[:, off:off + 512],
                        wvcs[b][:, dt * 128:(dt + 1) * 128],
                        ln1xt[:, dt * 2048 + cc * 512: dt * 2048 + (cc + 1) * 512],
                        start=(dt == 0), stop=(dt == 15)),
                        "eT" if dt == 15 else None)
                wv_used[b] = lastmm
                waits("scalar", lastmm)
                if cc == 0:
                    waits("scalar", vt_stored[b])
                if cc == 2:
                    waits("scalar", vt_stored[b])
                tev = opd("scalar", lambda E, b=b, cc=cc, pst=pst, off=off: E.activation(
                    vtsts[b][:, (cc % 2) * 512:(cc % 2) * 512 + 512],
                    pst[:, off:off + 512], AF.Identity), "eS")
                ps_free[cc] = tev
                if cc % 2 == 1:
                    waits("sync", tev)
                    half = cc // 2
                    vt_stored[b] = dma(
                        "sync",
                        vts[vc * 128:(vc + 1) * 128, half * 1024:(half + 1) * 1024],
                        vtsts[b][:, 0:1024])
                    vstore = vt_stored[b]
            castpass_step(2)
        esP2p.close()
        waits("scalar", vstore)
        tok_vall = None
        for kt in range(16):
            tok_vall = dma("scalar", vall[:, kt * 2048:(kt + 1) * 2048],
                           vts[:, kt * 128:(kt + 1) * 128], transpose=True, sem="qT")

        esB = ExitStack()
        wqh = [mksb(esB, f"wqh{i}", [128, 2048], BF16) for i in range(2)]
        wkh = [mksb(esB, f"wkh{i}", [128, 2048], BF16) for i in range(2)]
        ksb = mksb(esB, "ksb", [128, 2048], BF16)
        qsb = mksb(esB, "qsb", [128, 1024], BF16)
        kgs = mksb(esB, "kgs", [128, 32], F32)
        kgf = mksb(esB, "kgf", [128, 32], BF16)
        gsb = mksb(esB, "gsb", [128, 256], F32)
        th8 = mksb(esB, "th8", [128, 64], F32)
        msk = mksb(esB, "msk", [128, 256], F32)
        gb01 = mksb(esB, "gb01", [32, 1024], BF16)
        ptb = [mksb(esB, f"ptb{i}", [128, 1024], BF16) for i in range(2)]
        meb = [mksb(esB, f"meb{i}", [128, 1024], BF16) for i in range(4)]
        ceb = mksb(esB, "ceb", [128, 1024], F32)
        csb = mksb(esB, "csb", [1, 1024], F32)
        otst = mksb(esB, "otst", [128, 1024], BF16)

        wq_tok = [None, None]
        wq_used = [None, None]

        def prefetch_head(h_):
            b = h_ % 2
            waits("gpsimd", wq_used[b])
            dma("gpsimd", wqh[b][:],
                bass.AP(wq.tensor, h_ * 128,
                        [[2048, 128], [128 * 2048, 16], [1, 128]]))
            wq_tok[b] = dma("gpsimd", wkh[b][:],
                            bass.AP(wk.tensor, h_ * 128,
                                    [[2048, 128], [128 * 2048, 16], [1, 128]]))

        prefetch_head(0)

        pend_ot = [None]

        def flush_ot():
            if pend_ot[0] is None:
                return
            hprev, tce_p = pend_ot[0]
            pend_ot[0] = None
            waits("vector", tce_p, ot_store_t[0])
            ote = opd("vector", lambda E: E.tensor_mul(
                otst[:], psot[:], ceb[:]), "eV")
            ot_evac_t[0] = ote
            waits("sync", ote)
            ot_store_t[0] = dma("sync", ots[hprev, :, :], otst[:])

        ot_evac_t = [None]
        ot_store_t = [None]

        exp_tok = [None, None]
        mask_tok = [None, None]
        sp_tok = [None, None]
        chslot = [0]
        c_evac = None
        gb_evac = None
        me_last = None
        tce_last = None
        dbg_sig_tok = [None]
        dbg_gb_tok = [None]
        dbg_pt_tok = [None, None]
        dbg_me_tok_n = [None] * 4
        dbg_pt2_tok = [None, None]
        me_mask_done = [None] * 4

        for h_ in range(H):
            b = h_ % 2
            if h_ + 1 < H:
                prefetch_head(h_ + 1)
            waits("tensor", wq_tok[b], exp_tok[0], exp_tok[1], tok_vall)
            kevac = [None] * 4
            for cc in range(4):
                pst, off = slices4[cc]
                if cc >= 2:
                    waits("tensor", kevac[cc - 2])
                lastmm = None
                for dt in range(16):
                    lastmm = op("tensor", lambda E, b=b, dt=dt, cc=cc, pst=pst, off=off: E.matmul(
                        pst[:, off:off + 512],
                        wkh[b][:, dt * 128:(dt + 1) * 128],
                        ln1xt[:, dt * 2048 + cc * 512: dt * 2048 + (cc + 1) * 512],
                        start=(dt == 0), stop=(dt == 15)),
                        "eT" if dt == 15 else None)
                waits("scalar", lastmm)
                kevac[cc] = op("scalar", lambda E, cc=cc, pst=pst, off=off: E.activation(
                    ksb[:, cc * 512:(cc + 1) * 512], pst[:, off:off + 512],
                    AF.Identity), "eS")
            qevac = [None, None]
            for jg in range(2):
                pst, off = slices4[jg]
                waits("tensor", kevac[jg + 2])
                lastmm = None
                for dt in range(16):
                    rhs = bass.AP(ln1xt, dt * 2048 + jg * 1024,
                                  [[16 * 2048, 128], [256, 4], [1, 128]])
                    lastmm = op("tensor", lambda E, b=b, rhs=rhs, dt=dt, pst=pst, off=off: E.matmul(
                        pst[:, off:off + 512],
                        wqh[b][:, dt * 128:(dt + 1) * 128], rhs,
                        start=(dt == 0), stop=(dt == 15)),
                        "eT" if dt == 15 else None)
                wq_used[b] = lastmm
                waits("scalar", lastmm)
                qevac[jg] = op("scalar", lambda E, jg=jg, pst=pst, off=off: E.activation(
                    qsb[:, jg * 512:(jg + 1) * 512], pst[:, off:off + 512],
                    AF.Identity), "eS")
                if jg == 1:
                    qevac[jg] = op("scalar", lambda E: E.drain(), "eS")
            waits("vector", kevac[0], kevac[1], kevac[2], kevac[3])
            kv = ksb[:].rearrange("p (g c) -> p g c", c=64)
            op("vector", lambda E, kv=kv: E.tensor_reduce(
                kgs[:], kv, AX.X, ALU.add))
            bar("vector")
            tkg = opd("vector", lambda E: E.tensor_scalar(
                kgf[:], kgs[:], 1.0 / 64.0, None, ALU.mult), "eV")
            waits("tensor", qevac[0], qevac[1], tkg, sp_tok[0], sp_tok[1])
            gmm = [None] * NJ
            for j in range(NJ):
                gmm[j] = op("tensor", lambda E, j=j: E.matmul(
                    psb[:, j * 32:(j + 1) * 32],
                    qsb[:, j * 128:(j + 1) * 128], kgf[:],
                    start=True, stop=True), "eT")
            for j in range(NJ):
                waits("vector", gmm[j])
                op("vector", lambda E, j=j: E.tensor_add(
                    gsb[:, j * 32:(j + 1) * 32], psb[:, j * 32:(j + 1) * 32],
                    gm_sb[:, j * 32:(j + 1) * 32]))
            for j in range(NJ):
                op("vector", lambda E, j=j: E.max(
                    out=th8[:, j * 8:(j + 1) * 8], in_=gsb[:, j * 32:(j + 1) * 32]))
            for j in range(NJ):
                op("vector", lambda E, j=j: E.tensor_scalar(
                    msk[:, j * 32:(j + 1) * 32], gsb[:, j * 32:(j + 1) * 32],
                    th8[:, j * 8 + 7:j * 8 + 8], None, ALU.is_ge))
            tmv_l = [None] * NJ
            for j in range(NJ):
                tmv_l[j] = op("vector", lambda E, j=j: E.tensor_mul(
                    msk[:, j * 32:(j + 1) * 32], msk[:, j * 32:(j + 1) * 32],
                    cm_sb[:, j * 32:(j + 1) * 32]), "eV")
            last_tmv = op("vector", lambda E: E.drain(), "eV")
            tmv_l = [last_tmv] * NJ
            waits("scalar", me_last, dbg_gb_tok[0])
            for j in range(NJ):
                waits("tensor", tmv_l[j])
                ttr = op("tensor", lambda E, j=j: E.transpose(
                    psa[0:32, 0:128], msk[:, j * 32:(j + 1) * 32], c_idf[:]), "eT")
                waits("scalar", ttr)
                gb_evac = op("scalar", lambda E, j=j: E.activation(
                    gb01[0:32, j * 128:(j + 1) * 128], psa[0:32, 0:128],
                    AF.Identity), "eS")
                if j + 1 < NJ:
                    waits("tensor", gb_evac)
            gb_evac = op("scalar", lambda E: E.drain(), "eS")
            flush_ot()
            pspair = [psa, psb]

            def issue_me_bcast(kc):
                par4 = kc % 4
                jm_ = kc // 2
                qw_ = 1024 - jm_ * 128
                waits("scalar", me_mask_done[par4], gb_evac, dbg_me_tok_n[par4])
                t = dma("scalar", meb[par4][:, 0:qw_],
                        bass.AP(gb01, 2 * kc * 1024 + jm_ * 128,
                                [[1024, 2], [0, 64], [1, qw_]]), sem="qM")
                me_tok_n[kc] = t
                return t

            me_tok_n = [None] * 16
            for kc0 in range(4):
                tlast_me = issue_me_bcast(kc0)
            me_last = tlast_me
            pend = None
            for kc in range(16):
                par = kc % 2
                jm = kc // 2
                qw = 1024 - jm * 128
                nch = (qw + 511) // 512
                ch_slot = [None] * nch
                ch_mm = [None] * nch
                ch_te = [None] * nch
                for c in range(nch):
                    off = c * 512
                    n = min(512, qw - off)
                    slot = chslot[0]
                    chslot[0] ^= 1
                    ch_slot[c] = slot
                    pst = pspair[slot]
                    waits("tensor", exp_tok[slot])
                    if kc == 0:
                        waits("tensor", gb_evac, last_tmv)
                    stmm = op("tensor", lambda E, kc=kc, jm=jm, off=off, n=n, pst=pst: E.matmul(
                        pst[:, 0:n],
                        ksb[:, kc * 128:(kc + 1) * 128],
                        qsb[:, jm * 128 + off: jm * 128 + off + n],
                        start=True, stop=(kc % 2 == 1),
                        skip_group_check=True),
                        None if (kc % 2 == 0 and c == 0) else "eT")
                    if kc % 2 == 0 and c == 0:
                        stmm = op("tensor", lambda E, pst=pst: E.matmul(
                            pst[:, 0:128], c_i128[:], c_tri[:],
                            start=False, stop=True, skip_group_check=True), "eT")
                    ch_mm[c] = stmm
                if pend is not None:
                    pkc, ppar, pjm, pqw = pend
                    waits("tensor", mask_tok[ppar])
                    if pkc == 0:
                        waits("tensor", c_evac, ot_evac_t[0])
                    off = 0
                    last = None
                    while off < pqw:
                        n = min(512, pqw - off)
                        op("tensor", lambda E, pkc=pkc, pjm=pjm, off=off, n=n, ppar=ppar: E.matmul(
                            pssig[0:1, pjm * 128 + off: pjm * 128 + off + n],
                            c_ones[:], ptb[ppar][:, off:off + n],
                            start=(pkc == 0), stop=(pkc == 15),
                            skip_group_check=True))
                        last = op("tensor", lambda E, pkc=pkc, pjm=pjm, off=off, n=n, ppar=ppar, h_=h_: E.matmul(
                            psot[:, pjm * 128 + off: pjm * 128 + off + n],
                            vall[:, pkc * 2048 + h_ * 128: pkc * 2048 + (h_ + 1) * 128],
                            ptb[ppar][:, off:off + n],
                            start=(pkc == 0), stop=(pkc == 15),
                            skip_group_check=True), "eT" if off + n >= pqw else None)
                        off += n
                    sp_tok[ppar] = last
                for c in range(nch):
                    off = c * 512
                    n = min(512, qw - off)
                    pst = pspair[ch_slot[c]]
                    if c == 0:
                        waits("scalar", ch_mm[c], sp_tok[par], dbg_pt_tok[par])
                    else:
                        waits("scalar", ch_mm[c])
                    te = opd("scalar", lambda E, par=par, off=off, n=n, pst=pst: E.activation(
                        ptb[par][:, off:off + n], pst[:, 0:n],
                        AF.Exp, scale=SM), "eS")
                    exp_tok[ch_slot[c]] = te
                    ch_te[c] = te
                if DEBUG and h_ == 0:
                    waits("sync", ch_te[-1])
                    dbg_pt2_tok[par] = dma("sync", dbg_pt2[kc, :, 0:qw], ptb[par][:, 0:qw])
                waits("vector", dbg_pt2_tok[par])
                par4 = kc % 4
                tmk = None
                for c in range(nch):
                    off = c * 512
                    n = min(512, qw - off)
                    waits("vector", ch_te[c], me_tok_n[kc])
                    tmk = opd("vector", lambda E, par=par, par4=par4, off=off, n=n: E.tensor_mul(
                        ptb[par][:, off:off + n], ptb[par][:, off:off + n],
                        meb[par4][:, off:off + n]), "eV")
                mask_tok[par] = tmk
                me_mask_done[par4] = tmk
                if DEBUG and h_ == 0:
                    waits("sync", tmk)
                    dbg_pt_tok[par] = dma("sync", dbg_pt[kc, :, 0:qw], ptb[par][:, 0:qw])
                    dbg_me_tok_n[par4] = dma("sync", dbg_me[kc, :, 0:qw], meb[par4][:, 0:qw])
                if kc + 4 < 16:
                    me_last = issue_me_bcast(kc + 4)
                pend = (kc, par, jm, qw)
            pkc, ppar, pjm, pqw = pend
            waits("tensor", mask_tok[ppar])
            off = 0
            while off < pqw:
                n = min(512, pqw - off)
                op("tensor", lambda E, off=off, n=n, ppar=ppar: E.matmul(
                    pssig[0:1, pjm * 128 + off: pjm * 128 + off + n],
                    c_ones[:], ptb[ppar][:, off:off + n],
                    start=False, stop=True, skip_group_check=True))
                last = op("tensor", lambda E, off=off, n=n, ppar=ppar, h_=h_: E.matmul(
                    psot[:, pjm * 128 + off: pjm * 128 + off + n],
                    vall[:, 15 * 2048 + h_ * 128: 15 * 2048 + (h_ + 1) * 128],
                    ptb[ppar][:, off:off + n],
                    start=False, stop=True, skip_group_check=True),
                    "eT" if off + n >= pqw else None)
                off += n
            sp_tok[ppar] = last
            waits("vector", sp_tok[0], sp_tok[1], tce_last, dbg_sig_tok[0])
            if DEBUG:
                tsr = opd("vector", lambda E: E.tensor_copy(
                    ceb[0:1, 0:1024], pssig[0:1, :]), "eV")
                waits("sync", tsr)
                tsrd = dma("sync", dbg_sigr[h_, :], ceb[0:1, 0:1024])
                waits("vector", tsrd)
                bar("vector")
            c_evac = opd("vector", lambda E: E.reciprocal(
                csb[0:1, :], pssig[0:1, :]), "eV")
            waits("scalar", c_evac, ot_evac_t[0])
            tce = dma("scalar", ceb[:],
                      bass.AP(csb, 0, [[1024, 1], [0, 128], [1, 1024]]), sem="qM")
            tce_last = tce
            pend_ot[0] = (h_, tce)
            if DEBUG and h_ == 0:
                waits("scalar", me_mask_done[0], me_mask_done[1])
                t_me2a = dma("scalar", meb[0][:, 0:1024],
                             bass.AP(gb01, 0, [[1024, 2], [0, 64], [1, 1024]]))
                t_me2b = dma("scalar", meb[1][:, 0:1024],
                             bass.AP(gb01, 2 * 1024, [[1024, 2], [0, 64], [1, 1024]]))
                waits("sync", t_me2a, t_me2b)
                tm2a = dma("sync", dbg_me2[0], meb[0][:, 0:1024])
                tm2b = dma("sync", dbg_me2[1], meb[1][:, 0:1024])
                waits("scalar", tm2a, tm2b)
                me_mask_done[0] = ("qS", tm2a[1])
                me_mask_done[1] = ("qS", tm2b[1])
            if DEBUG:
                waits("sync", c_evac, gb_evac)
                tdsig = dma("sync", dbg_sig[h_, :], csb[0:1, :])
                dbg_sig_tok[0] = tdsig
                tdgb = dma("sync", dbg_gb[h_, :, :], gb01[:])
                dbg_gb_tok[0] = tdgb
                if h_ == H - 1:
                    dma("sync", dbg_k, ksb[:])
                    dma("sync", dbg_kg, kgf[:])
                    dma("sync", dbg_q, qsb[:])
                    dma("sync", dbg_gsb, gsb[:])
                    dma("sync", dbg_th8, th8[:])
            castpass_step(12)
        flush_ot()
        castpass_step(200)
        tok_ot_final = ot_store_t[0]
        tok_cp_all = ("qS", cnt["qS"])
        tok_cpw = ("qG", cnt["qG"])

        esB.close()
        esPS2.close()
        esA.close()
        psTR.close()

        esC = ExitStack()
        x1sb = mksb(esC, "x1sb", [128, 8 * 2048], F32)
        esC3 = ExitStack()
        wob = mksb(esC3, "wob", [128, 16 * 2048], BF16)
        otld = [mksb(esC3, f"otld{i}", [128, 16 * 256], BF16) for i in range(2)]
        xo = [mksb(esC3, f"xo{i}", [128, 2048], F32) for i in range(2)]
        ps3st = ExitStack()
        ps3 = mkps(ps3st, "ps3", [128, 4096])

        tok_attn_done = ("eT", cnt["eT"])
        waits("gpsimd", tok_attn_done)
        tok_wob = dma("gpsimd", wob[:],
                      bass.AP(wo.tensor, 0, [[2048, 128], [128 * 2048, 16], [1, 2048]]))
        waits("tensor", exp_tok[0], exp_tok[1], ot_evac_t[0], c_evac)

        ot_ld_used = [None, None]
        xo_used = [None, None]
        ps3_free = [None] * 8
        for jp in range(4):
            b = jp % 2
            waits("sync", tok_ot_final, ot_ld_used[b])
            tol = None
            for hd in range(H):
                tol = dma("sync", otld[b][:, hd * 256:(hd + 1) * 256],
                          ots[hd, :, jp * 256:(jp + 1) * 256])
            for jj in range(2):
                j = jp * 2 + jj
                waits("sync", xo_used[jj])
                txo = dma("sync", xo[jj][:], xb[2 * j * 128:(2 * j + 1) * 128, :])
                lastmm = [None] * 4
                waits("tensor", tok_wob, tol)
                for cc in range(4):
                    waits("tensor", ps3_free[jj * 4 + cc])
                for hd in range(H):
                    for cc in range(4):
                        sl = jj * 4 + cc
                        lastmm[cc] = op("tensor", lambda E, b=b, hd=hd, cc=cc, jj=jj, sl=sl: E.matmul(
                            ps3[:, sl * 512:(sl + 1) * 512],
                            otld[b][:, hd * 256 + jj * 128: hd * 256 + jj * 128 + 128],
                            wob[:, hd * 2048 + cc * 512: hd * 2048 + (cc + 1) * 512],
                            start=(hd == 0), stop=(hd == 15)),
                            "eT" if hd == 15 else None)
                ot_ld_used[b] = lastmm[3]
                for cc in range(4):
                    sl = jj * 4 + cc
                    waits("vector", lastmm[cc], txo)
                    tev = opd("vector", lambda E, j=j, cc=cc, sl=sl, jj=jj: E.tensor_add(
                        x1sb[:, j * 2048 + cc * 512: j * 2048 + (cc + 1) * 512],
                        ps3[:, sl * 512:(sl + 1) * 512],
                        xo[jj][:, cc * 512:(cc + 1) * 512]), "eV")
                    ps3_free[sl] = tev
                xo_used[jj] = ps3_free[jj * 4 + 3]

        ps3_toks = list(ps3_free)
        ps3st.close()
        esC3.close()

        esD = ExitStack()
        ln2xt = mksb(esD, "ln2xt", [128, 16 * 1024], BF16, side="right")
        esP4 = ExitStack()
        l2sts = [mksb(esP4, f"l2st{i}", [128, 2048], BF16) for i in range(2)]
        st2_done = [None, None]
        tok_ln2s = None
        prev_tv = None
        waits("scalar", cp_abs_used[0])
        for j in range(NJ):
            b = j % 2
            xv = x1sb[:, j * 2048:(j + 1) * 2048]
            for sl in range(8):
                waits("scalar", ps3_free[sl])
            waits("scalar", prev_tv)
            t1 = op("scalar", lambda E, xv=xv: E.activation(
                junkb[:], xv, AF.Identity, accum_out=stats[:, 0:1]), "eS")
            t2 = op("scalar", lambda E, xv=xv: E.activation(
                junkb[:], xv, AF.Square, accum_out=stats[:, 1:2]), "eS")
            waits("vector", t2)
            op("vector", lambda E: E.tensor_scalar(
                stats[:, 2:3], stats[:, 0:1], 1.0 / 2048.0, None, ALU.mult))
            op("vector", lambda E: E.tensor_scalar(
                stats[:, 3:4], stats[:, 1:2], 1.0 / 2048.0, None, ALU.mult))
            op("vector", lambda E: E.tensor_mul(
                stats[:, 4:5], stats[:, 2:3], stats[:, 2:3]))
            bar("vector")
            op("vector", lambda E: E.tensor_sub(
                stats[:, 5:6], stats[:, 3:4], stats[:, 4:5]))
            bar("vector")
            tv = op("vector", lambda E: E.tensor_scalar(
                stats[:, 5:6], stats[:, 5:6], 1.0, EPS_LN, ALU.mult, ALU.add), "eV")
            prev_tv = tv
            waits("scalar", tv)
            tq = op("scalar", lambda E: E.activation(
                stats[:, 6:7], stats[:, 5:6], AF.Sqrt), "eS")
            waits("vector", tq, st2_done[b])
            op("vector", lambda E: E.reciprocal(stats[:, 7:8], stats[:, 6:7]))
            bar("vector")
            op("vector", lambda E: E.scalar_tensor_tensor(
                stats[:, 8:9], stats[:, 2:3], -1.0, stats[:, 7:8],
                ALU.mult, ALU.mult))
            bar("vector")
            ta = opd("vector", lambda E, b=b, xv=xv: E.tensor_scalar(
                l2sts[b][:], xv, stats[:, 7:8], stats[:, 8:9],
                ALU.mult, ALU.add), "eV")
            waits("sync", ta)
            tok_ln2s = dma("sync", ln2s[j * 128:(j + 1) * 128, :], l2sts[b][:])
            st2_done[b] = tok_ln2s
            dma("sync", x1s[j * 128:(j + 1) * 128, :], xv)
        tok_x1_spill = ("qS", cnt["qS"])
        esP4.close()

        waits("scalar", tok_ln2s)
        tok_ln2xt = None
        for dt in range(16):
            tok_ln2xt = dma("scalar", ln2xt[:, dt * 1024:(dt + 1) * 1024],
                            ln2s[:, dt * 128:(dt + 1) * 128], transpose=True, sem="qT")

        esC.close()

        esE = ExitStack()
        ps5st = ExitStack()
        ps5 = mkps(ps5st, "ps5", [128, 4096])
        NQB = 4
        rq = [mksb(esE, f"rq{i}", [128, 512], BF16) for i in range(NQB)]
        sgb = mksb(esE, "sgb", [128, 8 * 512], BF16)
        sub2 = [mksb(esE, f"sub{i}", [128, 512], BF16) for i in range(2)]
        ht2 = [mksb(esE, f"htile{i}", [128, 512], BF16) for i in range(2)]
        htile = ht2[0]

        waits("scalar", tok_x1_spill, tok_cpw)
        waits("vector", tok_x1_spill)
        waits("gpsimd", tok_x1_spill)
        rq_used = [None] * NQB
        bank_mm = [None] * NJ
        bank_ev = [None] * NJ
        sg_used = [None] * NJ
        h_written = None
        htile_busy2 = [None, None]
        sub_busy2 = [None, None]
        h_written2 = [None, None]

        for hs in range(16):
            for gu_i in range(2):
                base_row = gu_i * HFF

                def issue_rd(dt):
                    i = dt % NQB
                    waits("scalar", rq_used[i])
                    rd_tok_n[dt] = dma(
                        "scalar", rq[i][:],
                        wgubT[dt * 128:(dt + 1) * 128,
                              base_row + hs * 512: base_row + (hs + 1) * 512],
                        sem="qW")

                rd_tok_n = [None] * 16
                for dt0 in range(3):
                    issue_rd(dt0)
                for dt in range(16):
                    i = dt % NQB
                    if dt + 3 < 16:
                        issue_rd(dt + 3)
                    waits("tensor", rd_tok_n[dt], tok_ln2xt)
                    lm = None
                    for j in range(NJ):
                        if dt == 0:
                            waits("tensor", bank_ev[j])
                        lm = op("tensor", lambda E, i=i, dt=dt, j=j: E.matmul(
                            ps5[:, j * 512:(j + 1) * 512],
                            ln2xt[:, dt * 1024 + j * 128: dt * 1024 + j * 128 + 128],
                            rq[i][:],
                            start=(dt == 0), stop=(dt == 15)),
                            "eT" if (j == NJ - 1) else None)
                    rq_used[i] = lm
                    if dt == 15:
                        for j in range(NJ):
                            bank_mm[j] = lm
                if gu_i == 0:
                    if DEBUG and hs == 0:
                        for j in range(NJ):
                            waits("vector", bank_mm[j])
                            tgd = opd("vector", lambda E, j=j: E.tensor_copy(
                                junkb[:, 0:1024].rearrange("p (a b) -> p (a b)") if False else htile[:],
                                ps5[:, j * 512:(j + 1) * 512]), "eV")
                            waits("sync", tgd)
                            tgd2 = dma("sync", dbg_gu[j], htile[:])
                            waits("vector", tgd2)
                    for j in range(NJ):
                        waits("scalar", bank_mm[j], sg_used[j])
                        bank_ev[j] = op("scalar", lambda E, j=j: E.activation(
                            sgb[:, j * 512:(j + 1) * 512],
                            ps5[:, j * 512:(j + 1) * 512], AF.Silu), "eS")
                    sil_drain = op("scalar", lambda E: E.drain(), "eS")
                    bank_ev = [sil_drain] * NJ
                    if DEBUG and hs == 0:
                        waits("sync", sil_drain)
                        for j in range(NJ):
                            dsg = dma("sync", dbg_sgb[j], sgb[:, j * 512:(j + 1) * 512])
                        waits("scalar", dsg)
                else:
                    for j in range(NJ):
                        b2 = j % 2
                        waits("scalar", bank_mm[j], sub_busy2[b2])
                        tcp = opd("scalar", lambda E, j=j, b2=b2: E.activation(
                            sub2[b2][:], ps5[:, j * 512:(j + 1) * 512],
                            AF.Identity), "eS")
                        bank_ev[j] = tcp
                        if DEBUG and hs == 0:
                            waits("sync", tcp)
                            tdu = dma("sync", dbg_up[j], sub2[b2][:])
                            waits("scalar", tdu)
                        waits("vector", tcp, htile_busy2[b2], h_written2[b2])
                        tmul = opd("vector", lambda E, j=j, b2=b2: E.tensor_mul(
                            ht2[b2][:], sgb[:, j * 512:(j + 1) * 512],
                            sub2[b2][:]), "eV")
                        sub_busy2[b2] = tmul
                        sg_used[j] = tmul
                        waits("scalar", tmul)
                        tsq = op("scalar", lambda E, j=j, hs=hs, b2=b2: E.activation(
                            junkb[:, 0:512], ht2[b2][:], AF.Square,
                            accum_out=ssqp[:, j * 16 + hs: j * 16 + hs + 1]), "eS")
                        htile_busy2[b2] = tsq
                        waits("sync", tmul)
                        if DEBUG and hs == 0:
                            tdh = dma("sync", dbg_ht[j], ht2[b2][:])
                            htile_busy2[b2] = ("qS", tdh[1])
                        h_written2[b2] = dma("sync",
                                             hhb[j * 128:(j + 1) * 128,
                                                 hs * 512:(hs + 1) * 512],
                                             ht2[b2][:])
        tok_h_all = ("qS", cnt["qS"])
        tok_sq_all = ("eS", cnt["eS"])
        tok_gu_done = ("eT", cnt["eT"])

        waits("vector", tok_sq_all)
        s3v = ssqp[:].rearrange("p (j c) -> p j c", c=16)
        op("vector", lambda E, s3v=s3v: E.tensor_reduce(
            stats[:, 16:24], s3v, AX.X, ALU.add))
        bar("vector")
        tv6 = op("vector", lambda E: E.tensor_scalar(
            stats[:, 16:24], stats[:, 16:24], 1.0 / float(HFF), EPS_RMS,
            ALU.mult, ALU.add), "eV")
        waits("scalar", tv6)
        tq6_ = op("scalar", lambda E: E.activation(
            invj[:], stats[:, 16:24], AF.Sqrt), "eS")
        waits("vector", tq6_)
        tinv = op("vector", lambda E: E.reciprocal(invj[:], invj[:]), "eV")

        ps5st.close()
        esE.close()
        esD_keep_stp = None
        esD.close()

        esF = ExitStack()
        ps7st = ExitStack()
        ps7 = [mkps(ps7st, "ps7a", [128, 2048]), mkps(ps7st, "ps7b", [128, 2048])]
        acc = mksb(esF, "acc", [128, 8 * 2048], F32)
        wqc = mksb(esF, "wqc", [128, 8 * 2048], BF16)
        htc = mksb(esF, "htc", [128, 8 * 1024], BF16)
        x1b = [mksb(esF, f"x1b{i}", [128, 2048], F32) for i in range(2)]
        ysb = [mksb(esF, f"ysb{i}", [128, 2048], F32) for i in range(2)]

        waits("scalar", tok_gu_done, tok_sq_all)
        waits("vector", tok_gu_done)
        waits("gpsimd", tok_gu_done)
        waits("tensor", tok_sq_all)
        wqc_used = None
        htc_used = None
        acc_done = [None] * NJ
        ps7_free = [None, None]

        for hcg in range(8):
            wq_toks = []
            ht_toks = []
            for i in range(8):
                hc = hcg * 8 + i
                waits("scalar", wqc_used)
                wq_toks.append(dma("scalar", wqc[:, i * 2048:(i + 1) * 2048],
                                   wprbT[hc * 128:(hc + 1) * 128, :], sem="qW"))
                waits("scalar", tok_h_all, htc_used)
                ht_toks.append(dma("scalar", htc[:, i * 1024:(i + 1) * 1024],
                                   hhb[:, hc * 128:(hc + 1) * 128], transpose=True, sem="qW"))
            for tki in wq_toks:
                waits("tensor", tki)
            waits("tensor", ht_toks[-1])
            lastmm = None
            for j in range(NJ):
                par = j % 2
                waits("tensor", ps7_free[par])
                lm = None
                for i in range(8):
                    for dc in range(4):
                        lm = op("tensor", lambda E, i=i, dc=dc, j=j, par=par: E.matmul(
                            ps7[par][:, dc * 512:(dc + 1) * 512],
                            htc[:, i * 1024 + j * 128: i * 1024 + j * 128 + 128],
                            wqc[:, i * 2048 + dc * 512: i * 2048 + (dc + 1) * 512],
                            start=(i == 0), stop=(i == 7)),
                            "eT" if (i == 7 and dc == 3) else None)
                lastmm = lm
                waits("vector", lm, acc_done[j])
                if hcg == 0:
                    tev = op("vector", lambda E, j=j, par=par: E.tensor_copy(
                        acc[:, j * 2048:(j + 1) * 2048], ps7[par][:]), "eV")
                else:
                    tev = op("vector", lambda E, j=j, par=par: E.tensor_add(
                        acc[:, j * 2048:(j + 1) * 2048],
                        acc[:, j * 2048:(j + 1) * 2048], ps7[par][:]), "eV")
                acc_done[j] = tev
                ps7_free[par] = tev
            wqc_used = lastmm
            htc_used = lastmm
            if DEBUG and hcg == 0:
                waits("sync", lastmm)
                dma("sync", dbg_wqc, wqc[:])
                dma("sync", dbg_htc, htc[:])
                for j in range(NJ):
                    waits("sync", acc_done[j])
                tac0 = dma("sync", dbg_acc0, acc[:])
                wqc_used = ("qS", tac0[1])
                htc_used = ("qS", tac0[1])

        if DEBUG:
            for j in range(NJ):
                waits("sync", acc_done[j])
            dma("sync", dbg_acc, acc[:])

        for t_ in acc_done:
            waits("sync", t_)
        y_tok = [None, None]
        for j in range(NJ):
            b = j % 2
            waits("sync", tok_x1_spill, y_tok[b])
            tx1 = dma("sync", x1b[b][:], x1s[j * 128:(j + 1) * 128, :])
            waits("vector", tx1, acc_done[j], tinv, y_tok[b])
            tfin = opd("vector", lambda E, j=j, b=b: E.scalar_tensor_tensor(
                ysb[b][:], acc[:, j * 2048:(j + 1) * 2048],
                invj[:, j:j + 1], x1b[b][:], ALU.mult, ALU.add), "eV")
            waits("sync", tfin)
            y_tok[b] = dma("sync", y[j * 128:(j + 1) * 128, :], ysb[b][:])

        if DEBUG:
            waits("gpsimd", ("qS", cnt["qS"]))
            dma("gpsimd", dbg_ln1, ln1s)
            dma("gpsimd", dbg_vts, vts)
            dma("gpsimd", dbg_ots, ots)
            dma("gpsimd", dbg_x1, x1s)
            dma("gpsimd", dbg_ln2, ln2s)
            dma("gpsimd", dbg_h, hhb)
            dma("gpsimd", dbg_ss, sscr)
            dma("gpsimd", dbg_wgub, wgubT)
            waits("sync", tinv)
            dma("sync", dbg_inv, invj[:])
            emit("gpsimd", lambda E, v=cnt["qG"]: E.wait_ge(semh["qG"], v))
        for nm_ in ("qS", "qA", "qG", "qM", "qW", "qT", "qY"):
            emit("sync", lambda E, nm_=nm_, v=cnt[nm_]: E.wait_ge(semh[nm_], v))

        ps7st.close()
        esF.close()

        with nc.Block() as block:
            @block.sync
            def _(E):
                for f in plan["sync"]:
                    f(E)

            @block.scalar
            def _(E):
                for f in plan["scalar"]:
                    f(E)

            @block.vector
            def _(E):
                for f in plan["vector"]:
                    f(E)

            @block.tensor
            def _(E):
                for f in plan["tensor"]:
                    f(E)

            @block.gpsimd
            def _(E):
                for f in plan["gpsimd"]:
                    f(E)

    return nc


def _host_masks(half):
    gm = np.zeros((NJ, 128, NB), np.float32)
    cm = np.zeros((NJ, 128, NB), np.float32)
    kb_abs = np.zeros(NB, np.int64)
    for u in range(16):
        a = (u ^ 1) if half == 1 else u
        kb_abs[2 * u] = 2 * a
        kb_abs[2 * u + 1] = 2 * a + 1
    for j in range(NJ):
        T = 2 * j + half
        for i in range(128):
            qb = (T * 128 + i) // 64
            gm[j, i][kb_abs == qb] = 1e30
            gm[j, i][kb_abs > qb] = -1e30
            cm[j, i] = (kb_abs <= qb).astype(np.float32)
    return gm, cm


def kernel(L0_attn_hidden_states, L0_attn_Wq, L0_attn_Wk, L0_attn_Wv,
           L0_attn_Wo, L0_ffn_w_gate_up, L0_ffn_w_proj):
    x = np.ascontiguousarray(np.asarray(L0_attn_hidden_states, dtype=np.float32))
    Wq = np.ascontiguousarray(np.asarray(L0_attn_Wq, dtype=np.float32))
    Wk = np.ascontiguousarray(np.asarray(L0_attn_Wk, dtype=np.float32))
    Wv = np.ascontiguousarray(np.asarray(L0_attn_Wv, dtype=np.float32))
    Wo = np.ascontiguousarray(np.asarray(L0_attn_Wo, dtype=np.float32))
    wgu_ = np.ascontiguousarray(np.asarray(L0_ffn_w_gate_up, dtype=np.float32))
    wpr_ = np.ascontiguousarray(np.asarray(L0_ffn_w_proj, dtype=np.float32))

    i128c = np.eye(128).astype(bfdt)
    idfc = np.eye(128).astype(np.float32)
    tric = np.where(np.arange(128)[:, None] > np.arange(128)[None, :],
                    NEG, 0.0).astype(bfdt)
    onesc = np.ones((128, 1)).astype(bfdt)

    nc = build_nc()
    in_maps = []
    for c in range(8):
        b, half = c // 2, c % 2
        xbv = x[b]
        if half == 1:
            xbv = xbv.reshape(16, 128, DM)[np.arange(16) ^ 1].reshape(S, DM)
        gm, cm = _host_masks(half)
        in_maps.append({
            "xb": np.ascontiguousarray(xbv),
            "wq": Wq, "wk": Wk, "wv": Wv, "wo": Wo,
            "wgu": wgu_, "wpr": wpr_,
            "gm": gm, "cmask": cm,
            "i128": i128c, "idf": idfc, "tri": tric, "ones1": onesc,
        })
    res = run_bass_kernel_spmd(nc, in_maps, core_ids=list(range(8)))

    out = np.empty((4, S, DM), np.float32)
    for c in range(8):
        b, half = c // 2, c % 2
        yc = np.asarray(res.results[c]["y"])
        if yc.dtype != np.float32:
            yc = yc.view(np.float32)
        yc = yc.reshape(NJ, 128, DM)
        for j in range(NJ):
            out[b, (2 * j + half) * 128:(2 * j + half + 1) * 128, :] = yc[j]
    return out


if __name__ == "__main__":
    import reference as _r
    ins = _r.setup_inputs()
    outs = kernel(**{k: np.asarray(v) for k, v in ins.items()})
    ref = np.asarray(_r.reference(**ins))
    err = np.abs(outs - ref)
    print("relL2:", float(np.sqrt((err ** 2).mean() / (ref ** 2).mean())),
          "max:", float(err.max()))


# version 06
# speedup vs baseline: 1.2702x; 1.0140x over previous
import os
import numpy as np
import ml_dtypes
import concourse.bass as bass
import concourse.mybir as mybir
from concourse.bass_utils import run_bass_kernel_spmd
from contextlib import ExitStack

F32 = mybir.dt.float32
BF16 = mybir.dt.bfloat16
AF = mybir.ActivationFunctionType
ALU = mybir.AluOpType
AX = mybir.AxisListType

H, D, S, DM = 16, 128, 2048, 2048
NB, BLK = 32, 64
HFF = 8192
NJ = 8
SM = float(D) ** -0.5
NEG = -1e31
EPS_LN = 1e-5
EPS_RMS = 1.1920929e-07
bfdt = ml_dtypes.bfloat16

ENGS = ("sync", "scalar", "vector", "tensor", "gpsimd")
RING = {"sync": "qS", "scalar": "qA", "gpsimd": "qG"}


DEBUG = os.environ.get("KDBG", "") == "1"


def build_nc():
    nc = bass.Bass("TRN2", target_bir_lowering=False, debug=False)

    xb = nc.dram_tensor("xb", [S, DM], F32, kind="ExternalInput").ap()
    wq = nc.dram_tensor("wq", [DM, DM], F32, kind="ExternalInput").ap()
    wk = nc.dram_tensor("wk", [DM, DM], F32, kind="ExternalInput").ap()
    wv = nc.dram_tensor("wv", [DM, DM], F32, kind="ExternalInput").ap()
    wo = nc.dram_tensor("wo", [DM, DM], F32, kind="ExternalInput").ap()
    wgu = nc.dram_tensor("wgu", [2 * HFF, DM], F32, kind="ExternalInput").ap()
    wpr = nc.dram_tensor("wpr", [DM, HFF], F32, kind="ExternalInput").ap()
    gmi = nc.dram_tensor("gm", [NJ, 128, NB], F32, kind="ExternalInput").ap()
    cmi = nc.dram_tensor("cmask", [NJ, 128, NB], F32, kind="ExternalInput").ap()
    i128 = nc.dram_tensor("i128", [128, 128], BF16, kind="ExternalInput").ap()
    idf = nc.dram_tensor("idf", [128, 128], F32, kind="ExternalInput").ap()
    tri = nc.dram_tensor("tri", [128, 128], BF16, kind="ExternalInput").ap()
    ones1 = nc.dram_tensor("ones1", [128, 1], BF16, kind="ExternalInput").ap()
    y = nc.dram_tensor("y", [NJ * 128, DM], F32, kind="ExternalOutput").ap()

    ln1s = nc.dram_tensor("ln1s", [S, DM], BF16, kind="Internal").ap()
    vts = nc.dram_tensor("vts", [DM, S], BF16, kind="Internal").ap()
    ots = nc.dram_tensor("ots", [H, 128, 1024], BF16, kind="Internal").ap()
    x1s = nc.dram_tensor("x1s", [1024, DM], F32, kind="Internal").ap()
    ln2s = nc.dram_tensor("ln2s", [1024, DM], BF16, kind="Internal").ap()
    wgubT = nc.dram_tensor("wgubT", [DM, 2 * HFF], BF16, kind="Internal").ap()
    wprbT = nc.dram_tensor("wprbT", [HFF, DM], BF16, kind="Internal").ap()
    sscr = nc.dram_tensor("sscr", [2 * HFF, 128], BF16, kind="Internal").ap()
    sscrp = nc.dram_tensor("sscrp", [DM, 128], BF16, kind="Internal").ap()
    sscrt = nc.dram_tensor("sscrt", [32, 2 * HFF], BF16, kind="Internal").ap()
    hhb = nc.dram_tensor("hhb", [1024, HFF], BF16, kind="Internal").ap()
    if DEBUG:
        dbg_ln1 = nc.dram_tensor("dbg_ln1", [S, DM], BF16, kind="ExternalOutput").ap()
        dbg_vts = nc.dram_tensor("dbg_vts", [DM, S], BF16, kind="ExternalOutput").ap()
        dbg_ots = nc.dram_tensor("dbg_ots", [H, 128, 1024], BF16, kind="ExternalOutput").ap()
        dbg_x1 = nc.dram_tensor("dbg_x1", [1024, DM], F32, kind="ExternalOutput").ap()
        dbg_ln2 = nc.dram_tensor("dbg_ln2", [1024, DM], BF16, kind="ExternalOutput").ap()
        dbg_h = nc.dram_tensor("dbg_h", [1024, HFF], BF16, kind="ExternalOutput").ap()
        dbg_ss = nc.dram_tensor("dbg_ss", [2 * HFF, 128], BF16, kind="ExternalOutput").ap()
        dbg_wgub = nc.dram_tensor("dbg_wgub", [DM, 2 * HFF], BF16, kind="ExternalOutput").ap()
        dbg_sig = nc.dram_tensor("dbg_sig", [H, 1024], F32, kind="ExternalOutput").ap()
        dbg_gb = nc.dram_tensor("dbg_gb", [H, 32, 1024], BF16, kind="ExternalOutput").ap()
        dbg_inv = nc.dram_tensor("dbg_inv", [128, 8], F32, kind="ExternalOutput").ap()
        dbg_k = nc.dram_tensor("dbg_k", [128, 2048], BF16, kind="ExternalOutput").ap()
        dbg_kg = nc.dram_tensor("dbg_kg", [128, 32], BF16, kind="ExternalOutput").ap()
        dbg_q = nc.dram_tensor("dbg_q", [128, 1024], BF16, kind="ExternalOutput").ap()
        dbg_gsb = nc.dram_tensor("dbg_gsb", [128, 256], F32, kind="ExternalOutput").ap()
        dbg_th8 = nc.dram_tensor("dbg_th8", [128, 64], F32, kind="ExternalOutput").ap()
        dbg_sigr = nc.dram_tensor("dbg_sigr", [H, 1024], F32, kind="ExternalOutput").ap()
        dbg_pt = nc.dram_tensor("dbg_pt", [16, 128, 1024], BF16, kind="ExternalOutput").ap()
        dbg_me = nc.dram_tensor("dbg_me", [16, 128, 1024], BF16, kind="ExternalOutput").ap()
        dbg_pt2 = nc.dram_tensor("dbg_pt2", [16, 128, 1024], BF16, kind="ExternalOutput").ap()
        dbg_me2 = nc.dram_tensor("dbg_me2", [2, 128, 1024], BF16, kind="ExternalOutput").ap()
        dbg_wq = nc.dram_tensor("dbg_wq", [16, 128, 512], BF16, kind="ExternalOutput").ap()
        dbg_se = nc.dram_tensor("dbg_se", [16, 128, 512], BF16, kind="ExternalOutput").ap()
        dbg_gu = nc.dram_tensor("dbg_gu", [8, 128, 512], BF16, kind="ExternalOutput").ap()
        dbg_sgb = nc.dram_tensor("dbg_sgb", [8, 128, 512], BF16, kind="ExternalOutput").ap()
        dbg_up = nc.dram_tensor("dbg_up", [8, 128, 512], BF16, kind="ExternalOutput").ap()
        dbg_ht = nc.dram_tensor("dbg_ht", [8, 128, 512], BF16, kind="ExternalOutput").ap()
        dbg_wqc = nc.dram_tensor("dbg_wqc", [128, 8 * 2048], BF16, kind="ExternalOutput").ap()
        dbg_htc = nc.dram_tensor("dbg_htc", [128, 8 * 1024], BF16, kind="ExternalOutput").ap()
        dbg_acc = nc.dram_tensor("dbg_acc", [128, 8 * 2048], F32, kind="ExternalOutput").ap()
        dbg_acc0 = nc.dram_tensor("dbg_acc0", [128, 8 * 2048], F32, kind="ExternalOutput").ap()
        dbg_prq = nc.dram_tensor("dbg_prq", [2, 128, 2048], BF16, kind="ExternalOutput").ap()
        dbg_sep = nc.dram_tensor("dbg_sep", [2, 128, 2048], BF16, kind="ExternalOutput").ap()
        dbg_pqa = nc.dram_tensor("dbg_pqa", [2, 128, 2048], BF16, kind="ExternalOutput").ap()

    plan = {e: [] for e in ENGS}
    cnt = {"qS": 0, "qA": 0, "qG": 0, "qM": 0, "qW": 0, "qT": 0, "qY": 0,
           "eT": 0, "eV": 0, "eS": 0, "eG": 0}
    RINGSEMS = ("qS", "qA", "qG", "qM", "qW", "qT", "qY")
    last_wait = {e: {k: 0 for k in cnt} for e in ENGS}
    semh = {}

    def emit(eng, fn):
        plan[eng].append(fn)

    def wait(eng, tok):
        if tok is None:
            return
        s, v = tok
        if s in RINGSEMS:
            v = cnt[s]
        if v <= 0 or last_wait[eng][s] >= v:
            return
        last_wait[eng][s] = v
        emit(eng, lambda E, s=s, v=v: E.wait_ge(semh[s], v))

    def waits(eng, *toks):
        for t in toks:
            wait(eng, t)

    def dma(eng, out, in_, transpose=False, sem=None):
        ring = sem if sem is not None else RING[eng]
        cnt[ring] += 16
        v = cnt[ring]

        def f(E, out=out, in_=in_, tr=transpose, s=ring):
            E.dma_start(out=out, in_=in_, transpose=tr).then_inc(semh[s], 16)
        emit(eng, f)
        return (ring, v)

    def bar(eng):
        emit(eng, lambda E: E.drain())

    def opd(eng, fn, sem):
        op(eng, fn)
        return op(eng, lambda E: E.drain(), sem)

    def op(eng, fn, sem=None):
        tok = None
        if sem is not None:
            cnt[sem] += 1
            tok = (sem, cnt[sem])

        def f(E, fn=fn, tok=tok):
            ins = fn(E)
            if tok is not None:
                ins.then_inc(semh[tok[0]], 1)
        emit(eng, f)
        return tok

    root = ExitStack()

    def mksb(st, name, shape, dtype, side=None):
        if side is None:
            return st.enter_context(nc.sbuf_tensor(name, shape, dtype))
        return st.enter_context(nc.sbuf_tensor(name, shape, dtype, side=side))

    def mkps(st, name, shape):
        return st.enter_context(nc.psum_tensor(name, shape, F32))

    with root:
        for nm in ("qS", "qA", "qG", "qM", "qW", "qT", "qY",
                   "eT", "eV", "eS", "eG"):
            semh[nm] = root.enter_context(nc.semaphore(nm))

        c_i128 = mksb(root, "c_i128", [128, 128], BF16)
        c_idf = mksb(root, "c_idf", [128, 128], F32)
        c_tri = mksb(root, "c_tri", [128, 128], BF16)
        c_ones = mksb(root, "c_ones", [128, 1], BF16)
        gm_sb = mksb(root, "gm_sb", [128, NJ * 32], F32)
        cm_sb = mksb(root, "cm_sb", [128, NJ * 32], F32)
        stats = mksb(root, "stats", [128, 32], F32)
        junkb = mksb(root, "junkb", [128, 2048], BF16)
        ssqp = mksb(root, "ssqp", [128, 128], F32)
        invj = mksb(root, "invj", [128, 8], F32)

        dma("sync", c_i128[:], i128)
        dma("sync", c_idf[:], idf)
        dma("sync", c_tri[:], tri)
        dma("sync", c_ones[:], ones1)
        dma("sync", gm_sb[:], bass.AP(gmi.tensor, 0, [[32, 128], [128 * 32, NJ], [1, 32]]))
        tok_consts = dma("sync", cm_sb[:], bass.AP(cmi.tensor, 0, [[32, 128], [128 * 32, NJ], [1, 32]]))

        esA = ExitStack()
        ln1xt = mksb(esA, "ln1xt", [128, 16 * 2048], BF16)
        vall = mksb(esA, "vall", [128, 16 * 2048], BF16)
        cp_nat = [mksb(esA, f"cp_nat{i}", [128, 2048], BF16) for i in range(2)]
        cp_abs = mksb(esA, "cp_abs", [128, 2048], BF16)
        cp_sum = mksb(esA, "cp_sum", [128, 32], F32)
        cp_s = mksb(esA, "cp_s", [128, 32], BF16)
        cp_stage = [mksb(esA, "cp_stage0", [128, 1024], BF16)]
        cp_s3 = mksb(esA, "cp_s3", [128, 2048], BF16)
        cp_m = cp_abs
        cp_ng = mksb(esA, "cp_ng", [128, 2048], BF16)
        psTR = ExitStack()
        pstr = [psTR.enter_context(nc.psum_tensor(f"pstr{i}", [128, 1024], BF16))
                for i in range(2)]

        cp_busy = [None, None]
        cp_busy2 = [None, None]
        cp_abs_used = [None]
        cp_sum_used = [None]
        cp_s_used = [None]
        cp_pos = [0]
        cp_last_store = [None]
        tbank_free = [None, None]
        stage_free = [None]
        tbank_i = [0]
        stage_i = [0]

        def castpass_unit(u):
            buf = u % 2
            if u < 128:
                src = wgu[u * 128:(u + 1) * 128, :]
                sdst = sscr[u * 128:(u + 1) * 128, 0:32]
                trow = 2 * HFF
                toff0 = u * 128
            else:
                v0 = u - 128
                dt, pc = v0 // 4, v0 % 4
                src = wpr[dt * 128:(dt + 1) * 128, pc * 2048:(pc + 1) * 2048]
                sdst = sscrp[dt * 128:(dt + 1) * 128, pc * 32:(pc + 1) * 32]
                trow = DM
                toff0 = None
            waits("gpsimd", cp_busy[buf], cp_busy2[buf])
            tin = dma("gpsimd", cp_nat[buf][:], src)
            waits("scalar", tin, cp_abs_used[0])
            ta = op("scalar", lambda E, b=buf: E.activation(
                cp_abs[:], cp_nat[b][:], AF.Abs, scale=2.0), "eS")
            waits("vector", ta, cp_sum_used[0])
            a3 = cp_abs[:].rearrange("p (g c) -> p g c", c=64)
            tr = op("vector", lambda E, a3=a3: E.tensor_reduce(
                cp_sum[:], a3, AX.X, ALU.add), "eV")
            waits("vector", tr, cp_s_used[0])
            bar("vector")
            tm = opd("vector", lambda E: E.tensor_scalar(
                cp_s[:], cp_sum[:], 1.0 / 128.0, 1e-8, ALU.mult, ALU.max), "eV")
            cp_sum_used[0] = tm
            waits("sync", tm)
            cp_s_used[0] = dma("sync", sdst, cp_s[:])
            cp_last_store[0] = cp_s_used[0]
            op("vector", lambda E: E.tensor_copy(
                bass.AP(cp_s3, 0, [[2048, 128], [1, 64], [64, 32]]),
                bass.AP(cp_s, 0, [[32, 128], [0, 64], [1, 32]])))
            for hh in range(2):
                op("vector", lambda E, b=buf, o=hh * 1024: E.tensor_scalar(
                    cp_ng[:, o:o + 1024], cp_nat[b][:, o:o + 1024],
                    0.0, None, ALU.is_lt))
            for hh in range(2):
                op("vector", lambda E, o=hh * 1024: E.tensor_tensor(
                    cp_m[:, o:o + 1024], cp_abs[:, o:o + 1024],
                    cp_s3[:, o:o + 1024], ALU.is_gt))
            for hh in range(2):
                op("vector", lambda E, o=hh * 1024: E.tensor_scalar(
                    cp_ng[:, o:o + 1024], cp_ng[:, o:o + 1024],
                    -2.0, 1.0, ALU.mult, ALU.add))
            for hh in range(2):
                op("vector", lambda E, o=hh * 1024: E.tensor_tensor(
                    cp_m[:, o:o + 1024], cp_m[:, o:o + 1024],
                    cp_s3[:, o:o + 1024], ALU.mult))
            tq = None
            for hh in range(2):
                tq = op("vector", lambda E, b=buf, o=hh * 1024: E.tensor_tensor(
                    cp_nat[b][:, o:o + 1024], cp_m[:, o:o + 1024],
                    cp_ng[:, o:o + 1024], ALU.mult), "eV" if hh else None)
            tq = op("vector", lambda E: E.drain(), "eV")
            cp_abs_used[0] = tq
            ttr = None
            for halfu in range(2):
                bk = tbank_i[0]
                tbank_i[0] ^= 1
                waits("tensor", tok_consts, tq, tbank_free[bk])
                for t8 in range(8):
                    t = halfu * 8 + t8
                    ttr = op("tensor", lambda E, bk=bk, t8=t8, t=t, b=buf: E.transpose(
                        pstr[bk][:, t8 * 128:(t8 + 1) * 128],
                        cp_nat[b][:, t * 128:(t + 1) * 128], c_i128[:]),
                        "eT" if t8 == 7 else None)
                sg = 0
                waits("vector", ttr, stage_free[sg])
                tev = opd("vector", lambda E, sg=sg, bk=bk: E.tensor_copy(
                    cp_stage[sg][:], pstr[bk][:]), "eV")
                tbank_free[bk] = tev
                waits("gpsimd", tev)
                if u < 128:
                    doff = halfu * 1024 * trow + u * 128
                    ddst = bass.AP(wgubT.tensor, doff,
                                   [[trow, 128], [128 * trow, 8], [1, 128]])
                else:
                    doff = (pc * 2048 + halfu * 1024) * trow + dt * 128
                    ddst = bass.AP(wprbT.tensor, doff,
                                   [[trow, 128], [128 * trow, 8], [1, 128]])
                tst = dma("gpsimd", ddst,
                          bass.AP(cp_stage[sg], 0, [[1024, 128], [128, 8], [1, 128]]))
                stage_free[sg] = tst
            cp_busy[buf] = ta
            cp_busy2[buf] = ttr

        def castpass_step(n):
            for _ in range(n):
                if cp_pos[0] >= 192:
                    return
                castpass_unit(cp_pos[0])
                cp_pos[0] += 1

        esP1 = ExitStack()
        xts = [mksb(esP1, f"xt{i}", [128, 2048], F32) for i in range(2)]
        lsts = [mksb(esP1, f"lst{i}", [128, 2048], BF16) for i in range(2)]
        ap_done = [None, None]
        st_done = [None, None]
        prev_tv = None
        for tc in range(16):
            b = tc % 2
            waits("sync", ap_done[b])
            tl = dma("sync", xts[b][:], xb[tc * 128:(tc + 1) * 128, :])
            waits("scalar", tl, prev_tv)
            t1 = op("scalar", lambda E, b=b: E.activation(
                junkb[:], xts[b][:], AF.Identity, accum_out=stats[:, 0:1]), "eS")
            t2 = op("scalar", lambda E, b=b: E.activation(
                junkb[:], xts[b][:], AF.Square, accum_out=stats[:, 1:2]), "eS")
            waits("vector", t2)
            op("vector", lambda E: E.tensor_scalar(
                stats[:, 2:3], stats[:, 0:1], 1.0 / 2048.0, None, ALU.mult))
            op("vector", lambda E: E.tensor_scalar(
                stats[:, 3:4], stats[:, 1:2], 1.0 / 2048.0, None, ALU.mult))
            op("vector", lambda E: E.tensor_mul(
                stats[:, 4:5], stats[:, 2:3], stats[:, 2:3]))
            bar("vector")
            op("vector", lambda E: E.tensor_sub(
                stats[:, 5:6], stats[:, 3:4], stats[:, 4:5]))
            bar("vector")
            tv = op("vector", lambda E: E.tensor_scalar(
                stats[:, 5:6], stats[:, 5:6], 1.0, EPS_LN, ALU.mult, ALU.add), "eV")
            prev_tv = tv
            waits("scalar", tv)
            tq = op("scalar", lambda E: E.activation(
                stats[:, 6:7], stats[:, 5:6], AF.Sqrt), "eS")
            waits("vector", tq)
            op("vector", lambda E: E.reciprocal(stats[:, 7:8], stats[:, 6:7]))
            bar("vector")
            op("vector", lambda E: E.scalar_tensor_tensor(
                stats[:, 8:9], stats[:, 2:3], -1.0, stats[:, 7:8],
                ALU.mult, ALU.mult))
            bar("vector")
            waits("vector", st_done[b])
            ta = opd("vector", lambda E, b=b: E.tensor_scalar(
                lsts[b][:], xts[b][:], stats[:, 7:8], stats[:, 8:9],
                ALU.mult, ALU.add), "eV")
            ap_done[b] = ta
            waits("sync", ta)
            st_done[b] = dma("sync", ln1s[tc * 128:(tc + 1) * 128, :], lsts[b][:])
            if tc % 3 == 2:
                castpass_step(1)
        tok_ln1_stores = st_done[:]
        esP1.close()

        waits("scalar", tok_ln1_stores[0], tok_ln1_stores[1], tok_consts)
        tok_ln1xt = None
        for dt in range(16):
            tok_ln1xt = dma("scalar", ln1xt[:, dt * 2048:(dt + 1) * 2048],
                            ln1s[:, dt * 128:(dt + 1) * 128], transpose=True, sem="qT")

        esPS2 = ExitStack()
        psa = mkps(esPS2, "psa", [128, 512])
        psb = mkps(esPS2, "psb", [128, 512])
        psot = mkps(esPS2, "psot", [128, 1024])
        pssig = mkps(esPS2, "pssig", [1, 1024])

        esP2p = ExitStack()
        wvcs = [mksb(esP2p, f"wvc{i}", [128, 2048], BF16) for i in range(2)]
        vtsts = [mksb(esP2p, f"vtst{i}", [128, 1024], BF16) for i in range(2)]
        wv_used = [None, None]
        vt_stored = [None, None]
        ps_free = [None] * 4
        slices4 = [(psa, 0), (psb, 0), (psa, 0), (psb, 0)]
        vstore = None
        for vc in range(16):
            b = vc % 2
            waits("gpsimd", wv_used[b])
            tw = dma("gpsimd", wvcs[b][:],
                     bass.AP(wv.tensor, vc * 128,
                             [[2048, 128], [128 * 2048, 16], [1, 128]]))
            for cc in range(4):
                pst, off = slices4[cc]
                waits("tensor", tw, tok_ln1xt, ps_free[cc], ps_free[(cc + 2) % 4])
                lastmm = None
                for dt in range(16):
                    lastmm = op("tensor", lambda E, b=b, dt=dt, cc=cc, pst=pst, off=off: E.matmul(
                        pst[:, off:off + 512],
                        wvcs[b][:, dt * 128:(dt + 1) * 128],
                        ln1xt[:, dt * 2048 + cc * 512: dt * 2048 + (cc + 1) * 512],
                        start=(dt == 0), stop=(dt == 15)),
                        "eT" if dt == 15 else None)
                wv_used[b] = lastmm
                waits("scalar", lastmm)
                if cc == 0:
                    waits("scalar", vt_stored[b])
                if cc == 2:
                    waits("scalar", vt_stored[b])
                tev = opd("scalar", lambda E, b=b, cc=cc, pst=pst, off=off: E.activation(
                    vtsts[b][:, (cc % 2) * 512:(cc % 2) * 512 + 512],
                    pst[:, off:off + 512], AF.Identity), "eS")
                ps_free[cc] = tev
                if cc % 2 == 1:
                    waits("sync", tev)
                    half = cc // 2
                    vt_stored[b] = dma(
                        "sync",
                        vts[vc * 128:(vc + 1) * 128, half * 1024:(half + 1) * 1024],
                        vtsts[b][:, 0:1024])
                    vstore = vt_stored[b]
            castpass_step(2)
        esP2p.close()
        waits("scalar", vstore)
        tok_vall = None
        for kt in range(16):
            tok_vall = dma("scalar", vall[:, kt * 2048:(kt + 1) * 2048],
                           vts[:, kt * 128:(kt + 1) * 128], transpose=True, sem="qT")

        esB = ExitStack()
        wqh = [mksb(esB, f"wqh{i}", [128, 2048], BF16) for i in range(2)]
        wkh = [mksb(esB, f"wkh{i}", [128, 2048], BF16) for i in range(2)]
        ksb = mksb(esB, "ksb", [128, 2048], BF16)
        qsb = mksb(esB, "qsb", [128, 1024], BF16)
        kgs = mksb(esB, "kgs", [128, 32], F32)
        kgf = mksb(esB, "kgf", [128, 32], BF16)
        gsb = mksb(esB, "gsb", [128, 256], F32)
        th8 = mksb(esB, "th8", [128, 64], F32)
        msk = mksb(esB, "msk", [128, 256], F32)
        gb01 = mksb(esB, "gb01", [32, 1024], BF16)
        ptb = [mksb(esB, f"ptb{i}", [128, 1024], BF16) for i in range(2)]
        meb = [mksb(esB, f"meb{i}", [128, 1024], BF16) for i in range(4)]
        ceb = mksb(esB, "ceb", [128, 1024], F32)
        csb = mksb(esB, "csb", [1, 1024], F32)
        otst = mksb(esB, "otst", [128, 1024], BF16)

        wq_tok = [None, None]
        wq_used = [None, None]

        def prefetch_head(h_):
            b = h_ % 2
            waits("gpsimd", wq_used[b])
            dma("gpsimd", wqh[b][:],
                bass.AP(wq.tensor, h_ * 128,
                        [[2048, 128], [128 * 2048, 16], [1, 128]]))
            wq_tok[b] = dma("gpsimd", wkh[b][:],
                            bass.AP(wk.tensor, h_ * 128,
                                    [[2048, 128], [128 * 2048, 16], [1, 128]]))

        prefetch_head(0)

        pend_ot = [None]

        def flush_ot():
            if pend_ot[0] is None:
                return
            hprev, tce_p = pend_ot[0]
            pend_ot[0] = None
            waits("vector", tce_p, ot_store_t[0])
            ote = opd("vector", lambda E: E.tensor_mul(
                otst[:], psot[:], ceb[:]), "eV")
            ot_evac_t[0] = ote
            waits("sync", ote)
            ot_store_t[0] = dma("sync", ots[hprev, :, :], otst[:])

        ot_evac_t = [None]
        ot_store_t = [None]

        exp_tok = [None, None]
        mask_tok = [None, None]
        sp_tok = [None, None]
        chslot = [0]
        c_evac = None
        gb_evac = None
        me_last = None
        tce_last = None
        dbg_sig_tok = [None]
        dbg_gb_tok = [None]
        dbg_pt_tok = [None, None]
        dbg_me_tok_n = [None] * 4
        dbg_pt2_tok = [None, None]
        me_mask_done = [None] * 4

        for h_ in range(H):
            b = h_ % 2
            if h_ + 1 < H:
                prefetch_head(h_ + 1)
            waits("tensor", wq_tok[b], exp_tok[0], exp_tok[1], tok_vall)
            kevac = [None] * 4
            for cc in range(4):
                pst, off = slices4[cc]
                if cc >= 2:
                    waits("tensor", kevac[cc - 2])
                lastmm = None
                for dt in range(16):
                    lastmm = op("tensor", lambda E, b=b, dt=dt, cc=cc, pst=pst, off=off: E.matmul(
                        pst[:, off:off + 512],
                        wkh[b][:, dt * 128:(dt + 1) * 128],
                        ln1xt[:, dt * 2048 + cc * 512: dt * 2048 + (cc + 1) * 512],
                        start=(dt == 0), stop=(dt == 15)),
                        "eT" if dt == 15 else None)
                waits("scalar", lastmm)
                kevac[cc] = op("scalar", lambda E, cc=cc, pst=pst, off=off: E.activation(
                    ksb[:, cc * 512:(cc + 1) * 512], pst[:, off:off + 512],
                    AF.Identity), "eS")
            qevac = [None, None]
            for jg in range(2):
                pst, off = slices4[jg]
                waits("tensor", kevac[jg + 2])
                lastmm = None
                for dt in range(16):
                    rhs = bass.AP(ln1xt, dt * 2048 + jg * 1024,
                                  [[16 * 2048, 128], [256, 4], [1, 128]])
                    lastmm = op("tensor", lambda E, b=b, rhs=rhs, dt=dt, pst=pst, off=off: E.matmul(
                        pst[:, off:off + 512],
                        wqh[b][:, dt * 128:(dt + 1) * 128], rhs,
                        start=(dt == 0), stop=(dt == 15)),
                        "eT" if dt == 15 else None)
                wq_used[b] = lastmm
                waits("scalar", lastmm)
                qevac[jg] = op("scalar", lambda E, jg=jg, pst=pst, off=off: E.activation(
                    qsb[:, jg * 512:(jg + 1) * 512], pst[:, off:off + 512],
                    AF.Identity), "eS")
                if jg == 1:
                    qevac[jg] = op("scalar", lambda E: E.drain(), "eS")
            waits("vector", kevac[0], kevac[1], kevac[2], kevac[3])
            kv = ksb[:].rearrange("p (g c) -> p g c", c=64)
            op("vector", lambda E, kv=kv: E.tensor_reduce(
                kgs[:], kv, AX.X, ALU.add))
            bar("vector")
            tkg = opd("vector", lambda E: E.tensor_scalar(
                kgf[:], kgs[:], 1.0 / 64.0, None, ALU.mult), "eV")
            waits("tensor", qevac[0], qevac[1], tkg, sp_tok[0], sp_tok[1])
            gmm = [None] * NJ
            for j in range(NJ):
                gmm[j] = op("tensor", lambda E, j=j: E.matmul(
                    psb[:, j * 32:(j + 1) * 32],
                    qsb[:, j * 128:(j + 1) * 128], kgf[:],
                    start=True, stop=True), "eT")
            for j in range(NJ):
                waits("vector", gmm[j])
                op("vector", lambda E, j=j: E.tensor_add(
                    gsb[:, j * 32:(j + 1) * 32], psb[:, j * 32:(j + 1) * 32],
                    gm_sb[:, j * 32:(j + 1) * 32]))
            for j in range(NJ):
                op("vector", lambda E, j=j: E.max(
                    out=th8[:, j * 8:(j + 1) * 8], in_=gsb[:, j * 32:(j + 1) * 32]))
            for j in range(NJ):
                op("vector", lambda E, j=j: E.tensor_scalar(
                    msk[:, j * 32:(j + 1) * 32], gsb[:, j * 32:(j + 1) * 32],
                    th8[:, j * 8 + 7:j * 8 + 8], None, ALU.is_ge))
            tmv_l = [None] * NJ
            for j in range(NJ):
                tmv_l[j] = op("vector", lambda E, j=j: E.tensor_mul(
                    msk[:, j * 32:(j + 1) * 32], msk[:, j * 32:(j + 1) * 32],
                    cm_sb[:, j * 32:(j + 1) * 32]), "eV")
            last_tmv = op("vector", lambda E: E.drain(), "eV")
            tmv_l = [last_tmv] * NJ
            waits("scalar", me_last, dbg_gb_tok[0])
            for j in range(NJ):
                waits("tensor", tmv_l[j])
                ttr = op("tensor", lambda E, j=j: E.transpose(
                    psa[0:32, 0:128], msk[:, j * 32:(j + 1) * 32], c_idf[:]), "eT")
                waits("scalar", ttr)
                gb_evac = op("scalar", lambda E, j=j: E.activation(
                    gb01[0:32, j * 128:(j + 1) * 128], psa[0:32, 0:128],
                    AF.Identity), "eS")
                if j + 1 < NJ:
                    waits("tensor", gb_evac)
            gb_evac = op("scalar", lambda E: E.drain(), "eS")
            flush_ot()
            pspair = [psa, psb]

            def issue_me_bcast(kc):
                par4 = kc % 4
                jm_ = kc // 2
                qw_ = 1024 - jm_ * 128
                waits("scalar", me_mask_done[par4], gb_evac, dbg_me_tok_n[par4])
                t = dma("scalar", meb[par4][:, 0:qw_],
                        bass.AP(gb01, 2 * kc * 1024 + jm_ * 128,
                                [[1024, 2], [0, 64], [1, qw_]]), sem="qM")
                me_tok_n[kc] = t
                return t

            me_tok_n = [None] * 16
            for kc0 in range(4):
                tlast_me = issue_me_bcast(kc0)
            me_last = tlast_me
            pend = None
            for kc in range(16):
                par = kc % 2
                jm = kc // 2
                qw = 1024 - jm * 128
                nch = (qw + 511) // 512
                ch_slot = [None] * nch
                ch_mm = [None] * nch
                ch_te = [None] * nch
                for c in range(nch):
                    off = c * 512
                    n = min(512, qw - off)
                    slot = chslot[0]
                    chslot[0] ^= 1
                    ch_slot[c] = slot
                    pst = pspair[slot]
                    waits("tensor", exp_tok[slot])
                    if kc == 0:
                        waits("tensor", gb_evac, last_tmv)
                    stmm = op("tensor", lambda E, kc=kc, jm=jm, off=off, n=n, pst=pst: E.matmul(
                        pst[:, 0:n],
                        ksb[:, kc * 128:(kc + 1) * 128],
                        qsb[:, jm * 128 + off: jm * 128 + off + n],
                        start=True, stop=(kc % 2 == 1),
                        skip_group_check=True),
                        None if (kc % 2 == 0 and c == 0) else "eT")
                    if kc % 2 == 0 and c == 0:
                        stmm = op("tensor", lambda E, pst=pst: E.matmul(
                            pst[:, 0:128], c_i128[:], c_tri[:],
                            start=False, stop=True, skip_group_check=True), "eT")
                    ch_mm[c] = stmm
                if pend is not None:
                    pkc, ppar, pjm, pqw = pend
                    waits("tensor", mask_tok[ppar])
                    if pkc == 0:
                        waits("tensor", c_evac, ot_evac_t[0])
                    off = 0
                    last = None
                    while off < pqw:
                        n = min(512, pqw - off)
                        op("tensor", lambda E, pkc=pkc, pjm=pjm, off=off, n=n, ppar=ppar: E.matmul(
                            pssig[0:1, pjm * 128 + off: pjm * 128 + off + n],
                            c_ones[:], ptb[ppar][:, off:off + n],
                            start=(pkc == 0), stop=(pkc == 15),
                            skip_group_check=True))
                        last = op("tensor", lambda E, pkc=pkc, pjm=pjm, off=off, n=n, ppar=ppar, h_=h_: E.matmul(
                            psot[:, pjm * 128 + off: pjm * 128 + off + n],
                            vall[:, pkc * 2048 + h_ * 128: pkc * 2048 + (h_ + 1) * 128],
                            ptb[ppar][:, off:off + n],
                            start=(pkc == 0), stop=(pkc == 15),
                            skip_group_check=True), "eT" if off + n >= pqw else None)
                        off += n
                    sp_tok[ppar] = last
                for c in range(nch):
                    off = c * 512
                    n = min(512, qw - off)
                    pst = pspair[ch_slot[c]]
                    if c == 0:
                        waits("scalar", ch_mm[c], sp_tok[par], dbg_pt_tok[par])
                    else:
                        waits("scalar", ch_mm[c])
                    te = opd("scalar", lambda E, par=par, off=off, n=n, pst=pst: E.activation(
                        ptb[par][:, off:off + n], pst[:, 0:n],
                        AF.Exp, scale=SM), "eS")
                    exp_tok[ch_slot[c]] = te
                    ch_te[c] = te
                if DEBUG and h_ == 0:
                    waits("sync", ch_te[-1])
                    dbg_pt2_tok[par] = dma("sync", dbg_pt2[kc, :, 0:qw], ptb[par][:, 0:qw])
                waits("vector", dbg_pt2_tok[par])
                par4 = kc % 4
                tmk = None
                for c in range(nch):
                    off = c * 512
                    n = min(512, qw - off)
                    waits("vector", ch_te[c], me_tok_n[kc])
                    tmk = opd("vector", lambda E, par=par, par4=par4, off=off, n=n: E.tensor_mul(
                        ptb[par][:, off:off + n], ptb[par][:, off:off + n],
                        meb[par4][:, off:off + n]), "eV")
                mask_tok[par] = tmk
                me_mask_done[par4] = tmk
                if DEBUG and h_ == 0:
                    waits("sync", tmk)
                    dbg_pt_tok[par] = dma("sync", dbg_pt[kc, :, 0:qw], ptb[par][:, 0:qw])
                    dbg_me_tok_n[par4] = dma("sync", dbg_me[kc, :, 0:qw], meb[par4][:, 0:qw])
                if kc + 4 < 16:
                    me_last = issue_me_bcast(kc + 4)
                pend = (kc, par, jm, qw)
            pkc, ppar, pjm, pqw = pend
            waits("tensor", mask_tok[ppar])
            off = 0
            while off < pqw:
                n = min(512, pqw - off)
                op("tensor", lambda E, off=off, n=n, ppar=ppar: E.matmul(
                    pssig[0:1, pjm * 128 + off: pjm * 128 + off + n],
                    c_ones[:], ptb[ppar][:, off:off + n],
                    start=False, stop=True, skip_group_check=True))
                last = op("tensor", lambda E, off=off, n=n, ppar=ppar, h_=h_: E.matmul(
                    psot[:, pjm * 128 + off: pjm * 128 + off + n],
                    vall[:, 15 * 2048 + h_ * 128: 15 * 2048 + (h_ + 1) * 128],
                    ptb[ppar][:, off:off + n],
                    start=False, stop=True, skip_group_check=True),
                    "eT" if off + n >= pqw else None)
                off += n
            sp_tok[ppar] = last
            waits("vector", sp_tok[0], sp_tok[1], tce_last, dbg_sig_tok[0])
            if DEBUG:
                tsr = opd("vector", lambda E: E.tensor_copy(
                    ceb[0:1, 0:1024], pssig[0:1, :]), "eV")
                waits("sync", tsr)
                tsrd = dma("sync", dbg_sigr[h_, :], ceb[0:1, 0:1024])
                waits("vector", tsrd)
                bar("vector")
            c_evac = opd("vector", lambda E: E.reciprocal(
                csb[0:1, :], pssig[0:1, :]), "eV")
            waits("scalar", c_evac, ot_evac_t[0])
            tce = dma("scalar", ceb[:],
                      bass.AP(csb, 0, [[1024, 1], [0, 128], [1, 1024]]), sem="qM")
            tce_last = tce
            pend_ot[0] = (h_, tce)
            if DEBUG and h_ == 0:
                waits("scalar", me_mask_done[0], me_mask_done[1])
                t_me2a = dma("scalar", meb[0][:, 0:1024],
                             bass.AP(gb01, 0, [[1024, 2], [0, 64], [1, 1024]]))
                t_me2b = dma("scalar", meb[1][:, 0:1024],
                             bass.AP(gb01, 2 * 1024, [[1024, 2], [0, 64], [1, 1024]]))
                waits("sync", t_me2a, t_me2b)
                tm2a = dma("sync", dbg_me2[0], meb[0][:, 0:1024])
                tm2b = dma("sync", dbg_me2[1], meb[1][:, 0:1024])
                waits("scalar", tm2a, tm2b)
                me_mask_done[0] = ("qS", tm2a[1])
                me_mask_done[1] = ("qS", tm2b[1])
            if DEBUG:
                waits("sync", c_evac, gb_evac)
                tdsig = dma("sync", dbg_sig[h_, :], csb[0:1, :])
                dbg_sig_tok[0] = tdsig
                tdgb = dma("sync", dbg_gb[h_, :, :], gb01[:])
                dbg_gb_tok[0] = tdgb
                if h_ == H - 1:
                    dma("sync", dbg_k, ksb[:])
                    dma("sync", dbg_kg, kgf[:])
                    dma("sync", dbg_q, qsb[:])
                    dma("sync", dbg_gsb, gsb[:])
                    dma("sync", dbg_th8, th8[:])
            castpass_step(16)
        flush_ot()
        castpass_step(200)
        tok_ot_final = ot_store_t[0]
        tok_cp_all = ("qS", cnt["qS"])
        tok_cpw = ("qG", cnt["qG"])

        esB.close()
        esPS2.close()
        esA.close()
        psTR.close()

        esC = ExitStack()
        x1sb = mksb(esC, "x1sb", [128, 8 * 2048], F32)
        esC3 = ExitStack()
        wob = mksb(esC3, "wob", [128, 16 * 2048], BF16)
        otld = [mksb(esC3, f"otld{i}", [128, 16 * 256], BF16) for i in range(2)]
        xo = [mksb(esC3, f"xo{i}", [128, 2048], F32) for i in range(2)]
        ps3st = ExitStack()
        ps3 = mkps(ps3st, "ps3", [128, 4096])

        tok_attn_done = ("eT", cnt["eT"])
        waits("gpsimd", tok_attn_done)
        tok_wob = dma("gpsimd", wob[:],
                      bass.AP(wo.tensor, 0, [[2048, 128], [128 * 2048, 16], [1, 2048]]))
        waits("tensor", exp_tok[0], exp_tok[1], ot_evac_t[0], c_evac)

        ot_ld_used = [None, None]
        xo_used = [None, None]
        ps3_free = [None] * 8
        for jp in range(4):
            b = jp % 2
            waits("sync", tok_ot_final, ot_ld_used[b])
            tol = None
            for hd in range(H):
                tol = dma("sync", otld[b][:, hd * 256:(hd + 1) * 256],
                          ots[hd, :, jp * 256:(jp + 1) * 256])
            for jj in range(2):
                j = jp * 2 + jj
                waits("sync", xo_used[jj])
                txo = dma("sync", xo[jj][:], xb[2 * j * 128:(2 * j + 1) * 128, :])
                lastmm = [None] * 4
                waits("tensor", tok_wob, tol)
                for cc in range(4):
                    waits("tensor", ps3_free[jj * 4 + cc])
                for hd in range(H):
                    for cc in range(4):
                        sl = jj * 4 + cc
                        lastmm[cc] = op("tensor", lambda E, b=b, hd=hd, cc=cc, jj=jj, sl=sl: E.matmul(
                            ps3[:, sl * 512:(sl + 1) * 512],
                            otld[b][:, hd * 256 + jj * 128: hd * 256 + jj * 128 + 128],
                            wob[:, hd * 2048 + cc * 512: hd * 2048 + (cc + 1) * 512],
                            start=(hd == 0), stop=(hd == 15)),
                            "eT" if hd == 15 else None)
                ot_ld_used[b] = lastmm[3]
                for cc in range(4):
                    sl = jj * 4 + cc
                    waits("vector", lastmm[cc], txo)
                    tev = opd("vector", lambda E, j=j, cc=cc, sl=sl, jj=jj: E.tensor_add(
                        x1sb[:, j * 2048 + cc * 512: j * 2048 + (cc + 1) * 512],
                        ps3[:, sl * 512:(sl + 1) * 512],
                        xo[jj][:, cc * 512:(cc + 1) * 512]), "eV")
                    ps3_free[sl] = tev
                xo_used[jj] = ps3_free[jj * 4 + 3]

        ps3_toks = list(ps3_free)
        ps3st.close()
        esC3.close()

        esD = ExitStack()
        ln2xt = mksb(esD, "ln2xt", [128, 16 * 1024], BF16, side="right")
        esP4 = ExitStack()
        l2sts = [mksb(esP4, f"l2st{i}", [128, 2048], BF16) for i in range(2)]
        st2_done = [None, None]
        tok_ln2s = None
        prev_tv = None
        waits("scalar", cp_abs_used[0])
        for j in range(NJ):
            b = j % 2
            xv = x1sb[:, j * 2048:(j + 1) * 2048]
            for sl in range(8):
                waits("scalar", ps3_free[sl])
            waits("scalar", prev_tv)
            t1 = op("scalar", lambda E, xv=xv: E.activation(
                junkb[:], xv, AF.Identity, accum_out=stats[:, 0:1]), "eS")
            t2 = op("scalar", lambda E, xv=xv: E.activation(
                junkb[:], xv, AF.Square, accum_out=stats[:, 1:2]), "eS")
            waits("vector", t2)
            op("vector", lambda E: E.tensor_scalar(
                stats[:, 2:3], stats[:, 0:1], 1.0 / 2048.0, None, ALU.mult))
            op("vector", lambda E: E.tensor_scalar(
                stats[:, 3:4], stats[:, 1:2], 1.0 / 2048.0, None, ALU.mult))
            op("vector", lambda E: E.tensor_mul(
                stats[:, 4:5], stats[:, 2:3], stats[:, 2:3]))
            bar("vector")
            op("vector", lambda E: E.tensor_sub(
                stats[:, 5:6], stats[:, 3:4], stats[:, 4:5]))
            bar("vector")
            tv = op("vector", lambda E: E.tensor_scalar(
                stats[:, 5:6], stats[:, 5:6], 1.0, EPS_LN, ALU.mult, ALU.add), "eV")
            prev_tv = tv
            waits("scalar", tv)
            tq = op("scalar", lambda E: E.activation(
                stats[:, 6:7], stats[:, 5:6], AF.Sqrt), "eS")
            waits("vector", tq, st2_done[b])
            op("vector", lambda E: E.reciprocal(stats[:, 7:8], stats[:, 6:7]))
            bar("vector")
            op("vector", lambda E: E.scalar_tensor_tensor(
                stats[:, 8:9], stats[:, 2:3], -1.0, stats[:, 7:8],
                ALU.mult, ALU.mult))
            bar("vector")
            ta = opd("vector", lambda E, b=b, xv=xv: E.tensor_scalar(
                l2sts[b][:], xv, stats[:, 7:8], stats[:, 8:9],
                ALU.mult, ALU.add), "eV")
            waits("sync", ta)
            tok_ln2s = dma("sync", ln2s[j * 128:(j + 1) * 128, :], l2sts[b][:])
            st2_done[b] = tok_ln2s
            dma("sync", x1s[j * 128:(j + 1) * 128, :], xv)
        tok_x1_spill = ("qS", cnt["qS"])
        esP4.close()

        waits("scalar", tok_ln2s)
        tok_ln2xt = None
        for dt in range(16):
            tok_ln2xt = dma("scalar", ln2xt[:, dt * 1024:(dt + 1) * 1024],
                            ln2s[:, dt * 128:(dt + 1) * 128], transpose=True, sem="qT")

        esC.close()

        esE = ExitStack()
        ps5st = ExitStack()
        ps5 = mkps(ps5st, "ps5", [128, 4096])
        NQB = 4
        rq = [mksb(esE, f"rq{i}", [128, 512], BF16) for i in range(NQB)]
        sgb = mksb(esE, "sgb", [128, 8 * 512], BF16)
        sub2 = [mksb(esE, f"sub{i}", [128, 512], BF16) for i in range(2)]
        ht2 = [mksb(esE, f"htile{i}", [128, 512], BF16) for i in range(2)]
        htile = ht2[0]

        waits("scalar", tok_x1_spill, tok_cpw)
        waits("vector", tok_x1_spill)
        waits("gpsimd", tok_x1_spill)
        rq_used = [None] * NQB
        bank_mm = [None] * NJ
        bank_ev = [None] * NJ
        sg_used = [None] * NJ
        h_written = None
        htile_busy2 = [None, None]
        sub_busy2 = [None, None]
        h_written2 = [None, None]

        for hs in range(16):
            for gu_i in range(2):
                base_row = gu_i * HFF

                def issue_rd(dt):
                    i = dt % NQB
                    waits("scalar", rq_used[i])
                    rd_tok_n[dt] = dma(
                        "scalar", rq[i][:],
                        wgubT[dt * 128:(dt + 1) * 128,
                              base_row + hs * 512: base_row + (hs + 1) * 512],
                        sem="qW")

                rd_tok_n = [None] * 16
                for dt0 in range(3):
                    issue_rd(dt0)
                for dt in range(16):
                    i = dt % NQB
                    if dt + 3 < 16:
                        issue_rd(dt + 3)
                    waits("tensor", rd_tok_n[dt], tok_ln2xt)
                    lm = None
                    for j in range(NJ):
                        if dt == 0:
                            waits("tensor", bank_ev[j])
                        lm = op("tensor", lambda E, i=i, dt=dt, j=j: E.matmul(
                            ps5[:, j * 512:(j + 1) * 512],
                            ln2xt[:, dt * 1024 + j * 128: dt * 1024 + j * 128 + 128],
                            rq[i][:],
                            start=(dt == 0), stop=(dt == 15)),
                            "eT" if (j == NJ - 1) else None)
                    rq_used[i] = lm
                    if dt == 15:
                        for j in range(NJ):
                            bank_mm[j] = lm
                if gu_i == 0:
                    if DEBUG and hs == 0:
                        for j in range(NJ):
                            waits("vector", bank_mm[j])
                            tgd = opd("vector", lambda E, j=j: E.tensor_copy(
                                junkb[:, 0:1024].rearrange("p (a b) -> p (a b)") if False else htile[:],
                                ps5[:, j * 512:(j + 1) * 512]), "eV")
                            waits("sync", tgd)
                            tgd2 = dma("sync", dbg_gu[j], htile[:])
                            waits("vector", tgd2)
                    for j in range(NJ):
                        waits("scalar", bank_mm[j], sg_used[j])
                        bank_ev[j] = op("scalar", lambda E, j=j: E.activation(
                            sgb[:, j * 512:(j + 1) * 512],
                            ps5[:, j * 512:(j + 1) * 512], AF.Silu), "eS")
                    sil_drain = op("scalar", lambda E: E.drain(), "eS")
                    bank_ev = [sil_drain] * NJ
                    if DEBUG and hs == 0:
                        waits("sync", sil_drain)
                        for j in range(NJ):
                            dsg = dma("sync", dbg_sgb[j], sgb[:, j * 512:(j + 1) * 512])
                        waits("scalar", dsg)
                else:
                    for j in range(NJ):
                        b2 = j % 2
                        waits("scalar", bank_mm[j], sub_busy2[b2])
                        tcp = opd("scalar", lambda E, j=j, b2=b2: E.activation(
                            sub2[b2][:], ps5[:, j * 512:(j + 1) * 512],
                            AF.Identity), "eS")
                        bank_ev[j] = tcp
                        if DEBUG and hs == 0:
                            waits("sync", tcp)
                            tdu = dma("sync", dbg_up[j], sub2[b2][:])
                            waits("scalar", tdu)
                        waits("vector", tcp, htile_busy2[b2], h_written2[b2])
                        tmul = opd("vector", lambda E, j=j, b2=b2: E.tensor_mul(
                            ht2[b2][:], sgb[:, j * 512:(j + 1) * 512],
                            sub2[b2][:]), "eV")
                        sub_busy2[b2] = tmul
                        sg_used[j] = tmul
                        waits("scalar", tmul)
                        tsq = op("scalar", lambda E, j=j, hs=hs, b2=b2: E.activation(
                            junkb[:, 0:512], ht2[b2][:], AF.Square,
                            accum_out=ssqp[:, j * 16 + hs: j * 16 + hs + 1]), "eS")
                        htile_busy2[b2] = tsq
                        waits("sync", tmul)
                        if DEBUG and hs == 0:
                            tdh = dma("sync", dbg_ht[j], ht2[b2][:])
                            htile_busy2[b2] = ("qS", tdh[1])
                        h_written2[b2] = dma("sync",
                                             hhb[j * 128:(j + 1) * 128,
                                                 hs * 512:(hs + 1) * 512],
                                             ht2[b2][:])
        tok_h_all = ("qS", cnt["qS"])
        tok_sq_all = ("eS", cnt["eS"])
        tok_gu_done = ("eT", cnt["eT"])

        waits("vector", tok_sq_all)
        s3v = ssqp[:].rearrange("p (j c) -> p j c", c=16)
        op("vector", lambda E, s3v=s3v: E.tensor_reduce(
            stats[:, 16:24], s3v, AX.X, ALU.add))
        bar("vector")
        tv6 = op("vector", lambda E: E.tensor_scalar(
            stats[:, 16:24], stats[:, 16:24], 1.0 / float(HFF), EPS_RMS,
            ALU.mult, ALU.add), "eV")
        waits("scalar", tv6)
        tq6_ = op("scalar", lambda E: E.activation(
            invj[:], stats[:, 16:24], AF.Sqrt), "eS")
        waits("vector", tq6_)
        tinv = op("vector", lambda E: E.reciprocal(invj[:], invj[:]), "eV")

        ps5st.close()
        esE.close()
        esD_keep_stp = None
        esD.close()

        esF = ExitStack()
        ps7st = ExitStack()
        ps7 = [mkps(ps7st, "ps7a", [128, 2048]), mkps(ps7st, "ps7b", [128, 2048])]
        acc = mksb(esF, "acc", [128, 8 * 2048], F32)
        wqc = mksb(esF, "wqc", [128, 8 * 2048], BF16)
        htc = mksb(esF, "htc", [128, 8 * 1024], BF16)
        x1b = [mksb(esF, f"x1b{i}", [128, 2048], F32) for i in range(2)]
        ysb = [mksb(esF, f"ysb{i}", [128, 2048], F32) for i in range(2)]

        waits("scalar", tok_gu_done, tok_sq_all)
        waits("vector", tok_gu_done)
        waits("gpsimd", tok_gu_done)
        waits("tensor", tok_sq_all)
        wqc_used = None
        htc_used = None
        acc_done = [None] * NJ
        ps7_free = [None, None]

        for hcg in range(8):
            wq_toks = []
            ht_toks = []
            for i in range(8):
                hc = hcg * 8 + i
                waits("scalar", wqc_used)
                wq_toks.append(dma("scalar", wqc[:, i * 2048:(i + 1) * 2048],
                                   wprbT[hc * 128:(hc + 1) * 128, :], sem="qW"))
                waits("scalar", tok_h_all, htc_used)
                ht_toks.append(dma("scalar", htc[:, i * 1024:(i + 1) * 1024],
                                   hhb[:, hc * 128:(hc + 1) * 128], transpose=True, sem="qW"))
            for tki in wq_toks:
                waits("tensor", tki)
            waits("tensor", ht_toks[-1])
            lastmm = None
            for j in range(NJ):
                par = j % 2
                waits("tensor", ps7_free[par])
                lm = None
                for i in range(8):
                    for dc in range(4):
                        lm = op("tensor", lambda E, i=i, dc=dc, j=j, par=par: E.matmul(
                            ps7[par][:, dc * 512:(dc + 1) * 512],
                            htc[:, i * 1024 + j * 128: i * 1024 + j * 128 + 128],
                            wqc[:, i * 2048 + dc * 512: i * 2048 + (dc + 1) * 512],
                            start=(i == 0), stop=(i == 7)),
                            "eT" if (i == 7 and dc == 3) else None)
                lastmm = lm
                waits("vector", lm, acc_done[j])
                if hcg == 0:
                    tev = op("vector", lambda E, j=j, par=par: E.tensor_copy(
                        acc[:, j * 2048:(j + 1) * 2048], ps7[par][:]), "eV")
                else:
                    tev = op("vector", lambda E, j=j, par=par: E.tensor_add(
                        acc[:, j * 2048:(j + 1) * 2048],
                        acc[:, j * 2048:(j + 1) * 2048], ps7[par][:]), "eV")
                acc_done[j] = tev
                ps7_free[par] = tev
            wqc_used = lastmm
            htc_used = lastmm
            if DEBUG and hcg == 0:
                waits("sync", lastmm)
                dma("sync", dbg_wqc, wqc[:])
                dma("sync", dbg_htc, htc[:])
                for j in range(NJ):
                    waits("sync", acc_done[j])
                tac0 = dma("sync", dbg_acc0, acc[:])
                wqc_used = ("qS", tac0[1])
                htc_used = ("qS", tac0[1])

        if DEBUG:
            for j in range(NJ):
                waits("sync", acc_done[j])
            dma("sync", dbg_acc, acc[:])

        for t_ in acc_done:
            waits("sync", t_)
        y_tok = [None, None]
        for j in range(NJ):
            b = j % 2
            waits("sync", tok_x1_spill, y_tok[b])
            tx1 = dma("sync", x1b[b][:], x1s[j * 128:(j + 1) * 128, :])
            waits("vector", tx1, acc_done[j], tinv, y_tok[b])
            tfin = opd("vector", lambda E, j=j, b=b: E.scalar_tensor_tensor(
                ysb[b][:], acc[:, j * 2048:(j + 1) * 2048],
                invj[:, j:j + 1], x1b[b][:], ALU.mult, ALU.add), "eV")
            waits("sync", tfin)
            y_tok[b] = dma("sync", y[j * 128:(j + 1) * 128, :], ysb[b][:])

        if DEBUG:
            waits("gpsimd", ("qS", cnt["qS"]))
            dma("gpsimd", dbg_ln1, ln1s)
            dma("gpsimd", dbg_vts, vts)
            dma("gpsimd", dbg_ots, ots)
            dma("gpsimd", dbg_x1, x1s)
            dma("gpsimd", dbg_ln2, ln2s)
            dma("gpsimd", dbg_h, hhb)
            dma("gpsimd", dbg_ss, sscr)
            dma("gpsimd", dbg_wgub, wgubT)
            waits("sync", tinv)
            dma("sync", dbg_inv, invj[:])
            emit("gpsimd", lambda E, v=cnt["qG"]: E.wait_ge(semh["qG"], v))
        for nm_ in ("qS", "qA", "qG", "qM", "qW", "qT", "qY"):
            emit("sync", lambda E, nm_=nm_, v=cnt[nm_]: E.wait_ge(semh[nm_], v))

        ps7st.close()
        esF.close()

        with nc.Block() as block:
            @block.sync
            def _(E):
                for f in plan["sync"]:
                    f(E)

            @block.scalar
            def _(E):
                for f in plan["scalar"]:
                    f(E)

            @block.vector
            def _(E):
                for f in plan["vector"]:
                    f(E)

            @block.tensor
            def _(E):
                for f in plan["tensor"]:
                    f(E)

            @block.gpsimd
            def _(E):
                for f in plan["gpsimd"]:
                    f(E)

    return nc


def _host_masks(half):
    gm = np.zeros((NJ, 128, NB), np.float32)
    cm = np.zeros((NJ, 128, NB), np.float32)
    kb_abs = np.zeros(NB, np.int64)
    for u in range(16):
        a = (u ^ 1) if half == 1 else u
        kb_abs[2 * u] = 2 * a
        kb_abs[2 * u + 1] = 2 * a + 1
    for j in range(NJ):
        T = 2 * j + half
        for i in range(128):
            qb = (T * 128 + i) // 64
            gm[j, i][kb_abs == qb] = 1e30
            gm[j, i][kb_abs > qb] = -1e30
            cm[j, i] = (kb_abs <= qb).astype(np.float32)
    return gm, cm


def kernel(L0_attn_hidden_states, L0_attn_Wq, L0_attn_Wk, L0_attn_Wv,
           L0_attn_Wo, L0_ffn_w_gate_up, L0_ffn_w_proj):
    x = np.ascontiguousarray(np.asarray(L0_attn_hidden_states, dtype=np.float32))
    Wq = np.ascontiguousarray(np.asarray(L0_attn_Wq, dtype=np.float32))
    Wk = np.ascontiguousarray(np.asarray(L0_attn_Wk, dtype=np.float32))
    Wv = np.ascontiguousarray(np.asarray(L0_attn_Wv, dtype=np.float32))
    Wo = np.ascontiguousarray(np.asarray(L0_attn_Wo, dtype=np.float32))
    wgu_ = np.ascontiguousarray(np.asarray(L0_ffn_w_gate_up, dtype=np.float32))
    wpr_ = np.ascontiguousarray(np.asarray(L0_ffn_w_proj, dtype=np.float32))

    i128c = np.eye(128).astype(bfdt)
    idfc = np.eye(128).astype(np.float32)
    tric = np.where(np.arange(128)[:, None] > np.arange(128)[None, :],
                    NEG, 0.0).astype(bfdt)
    onesc = np.ones((128, 1)).astype(bfdt)

    nc = build_nc()
    in_maps = []
    for c in range(8):
        b, half = c // 2, c % 2
        xbv = x[b]
        if half == 1:
            xbv = xbv.reshape(16, 128, DM)[np.arange(16) ^ 1].reshape(S, DM)
        gm, cm = _host_masks(half)
        in_maps.append({
            "xb": np.ascontiguousarray(xbv),
            "wq": Wq, "wk": Wk, "wv": Wv, "wo": Wo,
            "wgu": wgu_, "wpr": wpr_,
            "gm": gm, "cmask": cm,
            "i128": i128c, "idf": idfc, "tri": tric, "ones1": onesc,
        })
    res = run_bass_kernel_spmd(nc, in_maps, core_ids=list(range(8)))

    out = np.empty((4, S, DM), np.float32)
    for c in range(8):
        b, half = c // 2, c % 2
        yc = np.asarray(res.results[c]["y"])
        if yc.dtype != np.float32:
            yc = yc.view(np.float32)
        yc = yc.reshape(NJ, 128, DM)
        for j in range(NJ):
            out[b, (2 * j + half) * 128:(2 * j + half + 1) * 128, :] = yc[j]
    return out


if __name__ == "__main__":
    import reference as _r
    ins = _r.setup_inputs()
    outs = kernel(**{k: np.asarray(v) for k, v in ins.items()})
    ref = np.asarray(_r.reference(**ins))
    err = np.abs(outs - ref)
    print("relL2:", float(np.sqrt((err ** 2).mean() / (ref ** 2).mean())),
          "max:", float(err.max()))
